# version 03 (final)
# baseline (speedup 1.0000x reference)
import contextlib
import numpy as np
import ml_dtypes
import concourse.bass as bass
import concourse.mybir as mybir
from concourse.bass import broadcast_tensor_aps
from concourse.bass_utils import run_bass_kernel_spmd

F32 = mybir.dt.float32
BF16 = mybir.dt.bfloat16
AF = mybir.ActivationFunctionType
ALU = mybir.AluOpType
AX = mybir.AxisListType

NCORE = 8
D = 1024
B = 4
T = 4096
EPS = 1e-5


class SP:
    def __init__(self, nc):
        self.nc = nc
        self.items = []

    def c(self, eng, fn):
        self.items.append((eng, fn, 'c', 0))

    def d(self, eng, fn, n):
        self.items.append((eng, fn, 'd', n))

    def build(self):
        nc = self.nc
        with nc.semaphore("g") as g, nc.Block() as block:
            total = sum(1 if m == 'c' else 16 * n for (_, _, m, n) in self.items)

            def mk(engname):
                def body(eng):
                    cum = 0
                    for (e, fn, m, n) in self.items:
                        if e == engname:
                            if cum > 0:
                                eng.wait_ge(g, cum)
                            instrs = fn()
                            if not isinstance(instrs, (list, tuple)):
                                instrs = [instrs]
                            if m == 'c':
                                instrs[-1].then_inc(g, 1)
                                cum += 1
                            else:
                                assert len(instrs) == n, f"dma count {len(instrs)} != {n}"
                                for ins in instrs:
                                    ins.then_inc(g, 16)
                                cum += 16 * len(instrs)
                        else:
                            cum += 1 if m == 'c' else 16 * n
                    if engname == "vector":
                        eng.wait_ge(g, total)
                return body

            for name in ["sync", "scalar", "vector", "tensor", "gpsimd"]:
                getattr(block, name)(mk(name))


def ln_plain(sp, nc, out_ap, src_ap, width, t):
    mean, var, rstd, xc, sq, epsc = t['mean'], t['var'], t['rstd'], t['xc'], t['sq'], t['epsc']
    sp.c("vector", lambda: nc.vector.reduce_sum(mean[:, 0:1], src_ap, axis=AX.X))
    sp.c("scalar", lambda: nc.scalar.mul(mean[:, 0:1], mean[:, 0:1], 1.0 / width))
    sp.c("vector", lambda: nc.vector.tensor_scalar(xc[:, 0:width], src_ap, mean[:, 0:1], None, ALU.subtract))
    sp.c("vector", lambda: nc.vector.tensor_mul(sq[:, 0:width], xc[:, 0:width], xc[:, 0:width]))
    sp.c("vector", lambda: nc.vector.reduce_sum(var[:, 0:1], sq[:, 0:width], axis=AX.X))
    sp.c("scalar", lambda: nc.scalar.mul(var[:, 0:1], var[:, 0:1], 1.0 / width))
    sp.c("scalar", lambda: nc.scalar.activation(rstd[:, 0:1], var[:, 0:1], AF.Sqrt, bias=epsc[:, 0:1], scale=1.0))
    sp.c("vector", lambda: nc.vector.reciprocal(rstd[:, 0:1], rstd[:, 0:1]))
    sp.c("vector", lambda: nc.vector.tensor_scalar(out_ap, xc[:, 0:width], rstd[:, 0:1], None, ALU.mult))


def transpose_to(sp, nc, dst_fn, src_fn, nblk, tbk):
    idb = tbk['identb']
    pst = tbk['ps']
    j = 0
    while j < nblk:
        cnt = min(4, nblk - j)
        j0 = j

        def pe(j0=j0, cnt=cnt):
            return [nc.tensor.transpose(pst[:, i * 128:(i + 1) * 128], src_fn(j0 + i), idb[:, :])
                    for i in range(cnt)]
        sp.c("tensor", pe)

        def ev(j0=j0, cnt=cnt):
            return [nc.vector.tensor_copy(dst_fn(j0 + i), pst[:, i * 128:(i + 1) * 128]) for i in range(cnt)]
        sp.c("vector", ev)
        j += cnt


def softmax_rows(sp, nc, abf_ap, s_ap, t):
    mx, nmx, sm, rs = t['mean'], t['var'], t['rstd'], t['s2']
    sp.c("vector", lambda: nc.vector.reduce_max(mx[:, 0:1], s_ap, axis=AX.X))
    sp.c("scalar", lambda: nc.scalar.mul(nmx[:, 0:1], mx[:, 0:1], -1.0))
    sp.c("scalar", lambda: nc.scalar.activation(abf_ap, s_ap, AF.Exp, bias=nmx[:, 0:1], scale=1.0))
    sp.c("vector", lambda: nc.vector.reduce_sum(sm[:, 0:1], abf_ap, axis=AX.X))
    sp.c("vector", lambda: nc.vector.reciprocal(rs[:, 0:1], sm[:, 0:1]))
    sp.c("vector", lambda: nc.vector.tensor_scalar(abf_ap, abf_ap, rs[:, 0:1], None, ALU.mult))


def bcast_row(sp, nc, dst_ap, row_sb, width, ones_f, ps):
    off = 0
    while off < width:
        w = min(512, width - off)
        o = off
        sp.c("tensor", lambda o=o, w=w: nc.tensor.matmul(ps[:, 0:w], ones_f[0:1, :], row_sb[0:1, o:o + w], start=True, stop=True))
        sp.c("scalar", lambda o=o, w=w: nc.scalar.copy(dst_ap[:, o:o + w], ps[:, 0:w]))
        off += w


def load_w_bf16(sp, nc, dst_sb, w_dram, K, N, tmp_f32, col_off=0, nsplit=1):
    nk = (K + 127) // 128
    for c in range(nk):
        r0, r1 = c * 128, min((c + 1) * 128, K)
        pr = r1 - r0
        step = (N + nsplit - 1) // nsplit
        for s in range(nsplit):
            n0, n1 = s * step, min((s + 1) * step, N)
            o = col_off + c * N + n0
            sp.d("sync", lambda r0=r0, r1=r1, pr=pr, n0=n0, n1=n1: nc.sync.dma_start(out=tmp_f32[0:pr, 0:n1 - n0], in_=w_dram[r0:r1, n0:n1]), 1)
            sp.c("vector", lambda pr=pr, o=o, n0=n0, n1=n1: nc.vector.tensor_copy(dst_sb[0:pr, o:o + (n1 - n0)], tmp_f32[0:pr, 0:n1 - n0]))


def rope_apply(sp, nc, q_full, rot, ctab, stab, nheads, hd, tmp):
    half = hd // 2

    def neg_copy():
        src = q_full.rearrange("p (h j) -> p h j", j=hd)[:, :, half:hd]
        dst = rot[:, 0:nheads * hd].rearrange("p (h j) -> p h j", j=hd)[:, :, 0:half]
        return nc.scalar.mul(dst, src, -1.0)
    sp.c("scalar", neg_copy)

    def pos_copy():
        src = q_full.rearrange("p (h j) -> p h j", j=hd)[:, :, 0:half]
        dst = rot[:, 0:nheads * hd].rearrange("p (h j) -> p h j", j=hd)[:, :, half:hd]
        return nc.scalar.copy(dst, src)
    sp.c("scalar", pos_copy)

    def mul_s():
        rv = rot[:, 0:nheads * hd].rearrange("p (h j) -> p h j", j=hd)
        a, b = broadcast_tensor_aps(rv, stab[:, 0:hd].rearrange("p (h j) -> p h j", h=1))
        return nc.vector.tensor_tensor(rv, a, b, ALU.mult)
    sp.c("vector", mul_s)

    def mul_c():
        qv = q_full.rearrange("p (h j) -> p h j", j=hd)
        a, b = broadcast_tensor_aps(qv, ctab[:, 0:hd].rearrange("p (h j) -> p h j", h=1))
        return nc.vector.tensor_tensor(tmp[:, 0:nheads * hd].rearrange("p (h j) -> p h j", j=hd), a, b, ALU.mult)
    sp.c("vector", mul_c)
    sp.c("vector", lambda: nc.vector.tensor_add(q_full, tmp[:, 0:nheads * hd], rot[:, 0:nheads * hd]))


def head_bcast_mul(sp, nc, q_full, fac, nheads, hd):
    def f():
        qv = q_full.rearrange("p (h j) -> p h j", j=hd)
        a, b = broadcast_tensor_aps(qv, fac[:, 0:hd].rearrange("p (h j) -> p h j", h=1))
        return nc.vector.tensor_tensor(qv, a, b, ALU.mult)
    sp.c("vector", f)


def build_l0_attn():
    nc = bass.Bass("TRN2", target_bir_lowering=False)
    xb = nc.dram_tensor("xb", [T, D], F32, kind="ExternalInput").ap()
    xq = nc.dram_tensor("xq", [2048, D], F32, kind="ExternalInput").ap()
    wq = nc.dram_tensor("wq", [D, D], F32, kind="ExternalInput").ap()
    wk = nc.dram_tensor("wk", [D, D], F32, kind="ExternalInput").ap()
    wv = nc.dram_tensor("wv", [D, D], F32, kind="ExternalInput").ap()
    wo = nc.dram_tensor("wo", [D, D], F32, kind="ExternalInput").ap()
    bqr = nc.dram_tensor("bqr", [1, D], F32, kind="ExternalInput").ap()
    bkr = nc.dram_tensor("bkr", [1, D], F32, kind="ExternalInput").ap()
    bvr = nc.dram_tensor("bvr", [1, D], F32, kind="ExternalInput").ap()
    bor = nc.dram_tensor("bor", [1, D], F32, kind="ExternalInput").ap()
    ropeCq = nc.dram_tensor("ropeCq", [2048, 64], F32, kind="ExternalInput").ap()
    ropeSq = nc.dram_tensor("ropeSq", [2048, 64], F32, kind="ExternalInput").ap()
    ropeCk = nc.dram_tensor("ropeCk", [T, 64], F32, kind="ExternalInput").ap()
    ropeSk = nc.dram_tensor("ropeSk", [T, 64], F32, kind="ExternalInput").ap()
    onesd = nc.dram_tensor("onesd", [1, 128], F32, kind="ExternalInput").ap()
    identd = nc.dram_tensor("identd", [128, 128], BF16, kind="ExternalInput").ap()
    xo = nc.dram_tensor("xo", [2048, D], F32, kind="ExternalOutput").ap()

    QT_d = nc.dram_tensor("QT_d", [8, 128, 2048], BF16, kind="Internal").ap()
    KT_d = nc.dram_tensor("KT_d", [8, 128, T], BF16, kind="Internal").ap()
    V_d = nc.dram_tensor("V_d", [T, D], BF16, kind="Internal").ap()
    YT_d = nc.dram_tensor("YT_d", [8, 128, 2048], BF16, kind="Internal").ap()

    sp = SP(nc)
    st = contextlib.ExitStack()
    sb = lambda n, s, dt=F32: st.enter_context(nc.sbuf_tensor(n, s, dt))
    pst = lambda n, s, dt=F32: st.enter_context(nc.psum_tensor(n, s, dt))

    ones_f = sb("ones_f", [1, 128])
    identb = sb("identb", [128, 128], BF16)
    epsc = sb("epsc", [128, 1])
    warena = sb("warena", [128, 16384], BF16)
    wtmp = sb("wtmp", [128, 1024])
    brow = sb("brow", [1, D])
    biasb = sb("biasb", [128, 2 * D])
    xt = sb("xt", [128, D])
    xn = sb("xn", [128, D])
    xnb = sb("xnb", [128, D], BF16)
    xnT = sb("xnT", [128, 8 * 128], BF16)
    proj = sb("proj", [128, D])
    rot = sb("rot", [128, D])
    rtmp = sb("rtmp", [128, D])
    ctab = sb("ctab", [128, 64])
    stab = sb("stab", [128, 64])
    projb = sb("projb", [128, D], BF16)
    stage = sb("stage", [128, 8 * 128], BF16)
    QTp = sb("QTp", [128, 2048], BF16)
    KTp = sb("KTp", [128, T], BF16)
    Vp = sb("Vp", [128, 32 * 128], BF16)
    S_f = sb("S_f", [128, T])
    A_bf = sb("A_bf", [128, T], BF16)
    ATb = sb("ATb", [128, 16384], BF16)
    YTs = sb("YTs", [128, 2048], BF16)
    ytT = sb("ytT", [128, 8 * 128], BF16)
    yo = sb("yo", [128, D])
    t = dict(mean=sb("mean", [128, 1]), var=sb("var", [128, 1]), rstd=sb("rstd", [128, 1]),
             s2=sb("s2", [128, 1]), xc=sb("xc", [128, D]), sq=sb("sq", [128, D]), epsc=epsc)

    ps0 = pst("ps0", [128, 512])
    ps1 = pst("ps1", [128, 512])
    ps2 = pst("ps2", [128, 512])
    ps3 = pst("ps3", [128, 512])
    ps_y = pst("ps_y", [128, 512])
    ps_m = pst("ps_m", [128, 512])
    ps_tb = pst("ps_tb", [128, 512], BF16)
    tbk = {'identb': identb, 'ps': ps_tb}
    pslist = [ps0, ps1, ps2, ps3]

    sp.d("sync", lambda: [nc.sync.dma_start(out=ones_f[:, :], in_=onesd[:, :]),
                          nc.sync.dma_start(out=identb[:, :], in_=identd[:, :])], 2)
    sp.c("vector", lambda: nc.vector.memset(epsc[:, :], EPS))

    def bias_to(dst_off, b_dram):
        sp.d("sync", lambda: nc.sync.dma_start(out=brow[:, :], in_=b_dram[:, :]), 1)
        bcast_row(sp, nc, biasb[:, dst_off:dst_off + D], brow, D, ones_f, ps_m)

    def ln_tile(x_src, tile):
        r0 = tile * 128
        sp.d("sync", lambda: nc.sync.dma_start(out=xt[:, :], in_=x_src[r0:r0 + 128, :]), 1)
        ln_plain(sp, nc, xn[:, :], xt[:, :], D, t)
        sp.c("vector", lambda: nc.vector.tensor_copy(xnb[:, :], xn[:, :]))
        transpose_to(sp, nc, lambda j: xnT[:, j * 128:(j + 1) * 128],
                     lambda j: xnb[:, j * 128:(j + 1) * 128], 8, tbk)

    def project(wcol, bias_off, out_tile):
        for nh in range(2):
            def pe(nh=nh):
                return [nc.tensor.matmul((ps0 if nh == 0 else ps1)[:, :],
                                         xnT[:, c * 128:(c + 1) * 128],
                                         warena[:, wcol + c * 1024 + nh * 512: wcol + c * 1024 + nh * 512 + 512],
                                         start=(c == 0), stop=(c == 7)) for c in range(8)]
            sp.c("tensor", pe)
        for nh in range(2):
            sp.c("vector", lambda nh=nh: nc.vector.tensor_add(out_tile[:, nh * 512:nh * 512 + 512],
                                                              (ps0 if nh == 0 else ps1)[:, :],
                                                              biasb[:, bias_off + nh * 512: bias_off + nh * 512 + 512]))

    load_w_bf16(sp, nc, warena, wk, D, D, wtmp, 0)
    load_w_bf16(sp, nc, warena, wv, D, D, wtmp, 8192)
    bias_to(0, bkr)
    bias_to(D, bvr)
    for tile in range(32):
        ln_tile(xb, tile)
        project(0, 0, proj)
        sp.d("sync", lambda tile=tile: [nc.sync.dma_start(out=ctab[:, :], in_=ropeCk[tile * 128:tile * 128 + 128, :]),
                                        nc.sync.dma_start(out=stab[:, :], in_=ropeSk[tile * 128:tile * 128 + 128, :])], 2)
        rope_apply(sp, nc, proj[:, :], rot, ctab, stab, 16, 64, rtmp)
        sp.c("vector", lambda: nc.vector.tensor_copy(projb[:, :], proj[:, :]))
        transpose_to(sp, nc, lambda j: stage[:, j * 128:(j + 1) * 128],
                     lambda j: projb[:, j * 128:(j + 1) * 128], 8, tbk)

        def store_kt(tile=tile):
            src = stage[:, :].rearrange("p (r c) -> p r c", c=128)
            dst = KT_d[:, :, tile * 128:tile * 128 + 128].rearrange("r p c -> p r c")
            return nc.gpsimd.dma_start(out=dst, in_=src)
        sp.d("gpsimd", store_kt, 1)
        project(8192, D, proj)
        sp.c("vector", lambda: nc.vector.tensor_copy(projb[:, :], proj[:, :]))
        sp.d("gpsimd", lambda tile=tile: nc.gpsimd.dma_start(out=V_d[tile * 128:tile * 128 + 128, :], in_=projb[:, :]), 1)

    load_w_bf16(sp, nc, warena, wq, D, D, wtmp, 0)
    bias_to(0, bqr)
    for tile in range(16):
        ln_tile(xq, tile)
        project(0, 0, proj)
        sp.d("sync", lambda tile=tile: [nc.sync.dma_start(out=ctab[:, :], in_=ropeCq[tile * 128:tile * 128 + 128, :]),
                                        nc.sync.dma_start(out=stab[:, :], in_=ropeSq[tile * 128:tile * 128 + 128, :])], 2)
        rope_apply(sp, nc, proj[:, :], rot, ctab, stab, 16, 64, rtmp)
        sp.c("vector", lambda: nc.vector.tensor_copy(projb[:, :], proj[:, :]))
        transpose_to(sp, nc, lambda j: stage[:, j * 128:(j + 1) * 128],
                     lambda j: projb[:, j * 128:(j + 1) * 128], 8, tbk)

        def store_qt(tile=tile):
            src = stage[:, :].rearrange("p (r c) -> p r c", c=128)
            dst = QT_d[:, :, tile * 128:tile * 128 + 128].rearrange("r p c -> p r c")
            return nc.gpsimd.dma_start(out=dst, in_=src)
        sp.d("gpsimd", store_qt, 1)

    for pair in range(8):
        sp.d("sync", lambda pair=pair: [nc.sync.dma_start(out=QTp[:, :], in_=QT_d[pair, :, :]),
                                        nc.sync.dma_start(out=KTp[:, :], in_=KT_d[pair, :, :])], 2)

        def load_vp(pair=pair):
            src = V_d[:, pair * 128:(pair + 1) * 128].rearrange("(tt q) c -> q tt c", q=128)
            dst = Vp[:, :].rearrange("q (tt c) -> q tt c", c=128)
            return nc.sync.dma_start(out=dst, in_=src)
        sp.d("sync", load_vp, 1)
        for hh in range(2):
            hb = hh * 64
            for g in range(4):
                for qt in range(4):
                    qc = g * 512 + qt * 128
                    for kh in range(2):
                        def pe(kh=kh, qc=qc, hb=hb):
                            return [nc.tensor.matmul(pslist[j][:, :],
                                                     QTp[hb:hb + 64, qc:qc + 128],
                                                     KTp[hb:hb + 64, (kh * 4 + j) * 512:(kh * 4 + j) * 512 + 512],
                                                     start=True, stop=True) for j in range(4)]
                        sp.c("tensor", pe)

                        def ev(kh=kh):
                            return [nc.scalar.copy(S_f[:, (kh * 4 + j) * 512:(kh * 4 + j) * 512 + 512], pslist[j][:, :])
                                    for j in range(4)]
                        sp.c("scalar", ev)
                    softmax_rows(sp, nc, A_bf[:, :], S_f[:, :], t)
                    transpose_to(sp, nc,
                                 lambda j, qt=qt: ATb[:, j * 512 + qt * 128: j * 512 + qt * 128 + 128],
                                 lambda j: A_bf[:, j * 128:(j + 1) * 128], 32, tbk)

                def av(hb=hb):
                    return [nc.tensor.matmul(ps_y[0:64, :], Vp[:, kt * 128 + hb: kt * 128 + hb + 64],
                                             ATb[:, kt * 512:(kt + 1) * 512],
                                             start=(kt == 0), stop=(kt == 31)) for kt in range(32)]
                sp.c("tensor", av)
                sp.c("scalar", lambda hb=hb, g=g: nc.scalar.copy(YTs[hb:hb + 64, g * 512:(g + 1) * 512], ps_y[0:64, :]))
        sp.d("gpsimd", lambda pair=pair: nc.gpsimd.dma_start(out=YT_d[pair, :, :], in_=YTs[:, :]), 1)

    load_w_bf16(sp, nc, warena, wo, D, D, wtmp, 0)
    bias_to(0, bor)
    for tile in range(16):
        def load_yt(tile=tile):
            src = YT_d[:, :, tile * 128:tile * 128 + 128].rearrange("r p c -> p r c")
            dst = ytT[:, :].rearrange("p (r c) -> p r c", c=128)
            return nc.sync.dma_start(out=dst, in_=src)
        sp.d("sync", load_yt, 1)
        for nh in range(2):
            def pe(nh=nh):
                return [nc.tensor.matmul((ps0 if nh == 0 else ps1)[:, :],
                                         ytT[:, c * 128:(c + 1) * 128],
                                         warena[:, c * 1024 + nh * 512: c * 1024 + nh * 512 + 512],
                                         start=(c == 0), stop=(c == 7)) for c in range(8)]
            sp.c("tensor", pe)
        for nh in range(2):
            sp.c("vector", lambda nh=nh: nc.vector.tensor_add(yo[:, nh * 512:nh * 512 + 512],
                                                              (ps0 if nh == 0 else ps1)[:, :],
                                                              biasb[:, nh * 512:nh * 512 + 512]))
        ln_plain(sp, nc, yo[:, :], yo[:, :], D, t)
        sp.d("sync", lambda tile=tile: nc.sync.dma_start(out=xt[:, :], in_=xq[tile * 128:tile * 128 + 128, :]), 1)
        sp.c("vector", lambda: nc.vector.tensor_add(yo[:, :], yo[:, :], xt[:, :]))
        sp.d("gpsimd", lambda tile=tile: nc.gpsimd.dma_start(out=xo[tile * 128:tile * 128 + 128, :], in_=yo[:, :]), 1)

    sp.build()
    st.close()
    return nc


def build_l0_ffn():
    nc = bass.Bass("TRN2", target_bir_lowering=False)
    NT = 2048
    x1 = nc.dram_tensor("x1", [NT, D], F32, kind="ExternalInput").ap()
    ln_wd = nc.dram_tensor("ln_wd", [1, D], F32, kind="ExternalInput").ap()
    ln_bd = nc.dram_tensor("ln_bd", [1, D], F32, kind="ExternalInput").ap()
    router_w = nc.dram_tensor("router_w", [D, 8], F32, kind="ExternalInput").ap()
    gate_w = nc.dram_tensor("gate_w", [8, D, 2048], F32, kind="ExternalInput").ap()
    up_w = nc.dram_tensor("up_w", [8, D, 2048], F32, kind="ExternalInput").ap()
    down_w = nc.dram_tensor("down_w", [8, 2048, D], F32, kind="ExternalInput").ap()
    sh_gate = nc.dram_tensor("sh_gate", [D, 2048], F32, kind="ExternalInput").ap()
    sh_up = nc.dram_tensor("sh_up", [D, 2048], F32, kind="ExternalInput").ap()
    sh_down = nc.dram_tensor("sh_down", [2048, D], F32, kind="ExternalInput").ap()
    onesd = nc.dram_tensor("onesd", [1, 128], F32, kind="ExternalInput").ap()
    identd = nc.dram_tensor("identd", [128, 128], BF16, kind="ExternalInput").ap()
    identfd = nc.dram_tensor("identfd", [128, 128], F32, kind="ExternalInput").ap()
    xo = nc.dram_tensor("xo", [NT, D], F32, kind="ExternalOutput").ap()

    h_d = nc.dram_tensor("h_d", [NT, D], F32, kind="Internal").ap()
    XFT_d = nc.dram_tensor("XFT_d", [8, 128, NT], BF16, kind="Internal").ap()
    wT_d = nc.dram_tensor("wT_d", [8, NT], F32, kind="Internal").ap()

    sp = SP(nc)
    st = contextlib.ExitStack()
    sb = lambda n, s, dt=F32: st.enter_context(nc.sbuf_tensor(n, s, dt))
    pst = lambda n, s, dt=F32: st.enter_context(nc.psum_tensor(n, s, dt))

    ones_f = sb("ones_f", [1, 128])
    identb = sb("identb", [128, 128], BF16)
    identf = sb("identf", [128, 128], F32)
    epsc = sb("epsc", [128, 1])
    g_b = sb("g_b", [128, 8 * 2048], BF16)
    u_b = sb("u_b", [128, 8 * 2048], BF16)
    d_b = sb("d_b", [128, 16 * 1024], BF16)
    acc = sb("acc", [128, 16 * 1024], BF16)
    XFTg = sb("XFTg", [128, 8 * 512], BF16)
    hm = sb("hm", [128, 16 * 512], BF16)
    wB = sb("wB", [128, 512])
    wrow = sb("wrow", [1, 512])
    sb_a = sb("sb_a", [128, 512])
    sb_b = sb("sb_b", [128, 512])
    sb_c = sb("sb_c", [128, 512], BF16)
    wtmp = sb("wtmp", [128, 2048])
    lrow = sb("lrow", [1, D])
    xt = sb("xt", [128, D])
    hT = sb("hT", [128, D])
    xf = sb("xf", [128, D])
    xfb = sb("xfb", [128, D], BF16)
    xnT = sb("xnT", [128, 8 * 128], BF16)
    lnwB = sb("lnwB", [128, D])
    lnbB = sb("lnbB", [128, D])
    rw = sb("rw", [128, 64], BF16)
    r8 = sb("r8", [8, 128])
    lg = sb("lg", [128, 8])
    wcols = sb("wcols", [128, 8])
    sig = sb("sig", [128, 1])
    t = dict(mean=sb("mean", [128, 1]), var=sb("var", [128, 1]), rstd=sb("rstd", [128, 1]),
             s2=sb("s2", [128, 1]), xc=sb("xc", [128, D]), sq=sb("sq", [128, D]), epsc=epsc)

    ps0 = pst("ps0", [128, 512])
    ps1 = pst("ps1", [128, 512])
    ps2 = pst("ps2", [128, 512])
    ps3 = pst("ps3", [128, 512])
    ps_m = pst("ps_m", [128, 512])
    ps_tb = pst("ps_tb", [128, 512], BF16)
    tbk = {'identb': identb, 'ps': ps_tb}

    sp.d("sync", lambda: [nc.sync.dma_start(out=ones_f[:, :], in_=onesd[:, :]),
                          nc.sync.dma_start(out=identb[:, :], in_=identd[:, :]),
                          nc.sync.dma_start(out=identf[:, :], in_=identfd[:, :])], 3)
    sp.c("vector", lambda: nc.vector.memset(epsc[:, :], EPS))
    sp.d("sync", lambda: nc.sync.dma_start(out=lrow[:, :], in_=ln_wd[:, :]), 1)
    bcast_row(sp, nc, lnwB[:, :], lrow, D, ones_f, ps_m)
    sp.d("sync", lambda: nc.sync.dma_start(out=lrow[:, :], in_=ln_bd[:, :]), 1)
    bcast_row(sp, nc, lnbB[:, :], lrow, D, ones_f, ps_m)
    sp.d("sync", lambda: nc.sync.dma_start(out=wtmp[:, 0:64].rearrange("p (c e) -> p c e", e=8),
                                           in_=router_w.rearrange("(c p) e -> p c e", p=128)), 1)
    sp.c("vector", lambda: nc.vector.tensor_copy(rw[:, :], wtmp[:, 0:64]))

    for tile in range(16):
        r0 = tile * 128
        sp.d("sync", lambda r0=r0: nc.sync.dma_start(out=xt[:, :], in_=x1[r0:r0 + 128, :]), 1)
        ln_plain(sp, nc, hT[:, :], xt[:, :], D, t)
        sp.d("gpsimd", lambda r0=r0: nc.gpsimd.dma_start(out=h_d[r0:r0 + 128, :], in_=hT[:, :]), 1)
        ln_plain(sp, nc, xf[:, :], hT[:, :], D, t)
        sp.c("vector", lambda: nc.vector.tensor_mul(xf[:, :], xf[:, :], lnwB[:, :]))
        sp.c("vector", lambda: nc.vector.tensor_add(xf[:, :], xf[:, :], lnbB[:, :]))
        sp.c("vector", lambda: nc.vector.tensor_copy(xfb[:, :], xf[:, :]))
        transpose_to(sp, nc, lambda j: xnT[:, j * 128:(j + 1) * 128],
                     lambda j: xfb[:, j * 128:(j + 1) * 128], 8, tbk)

        def store_xft(r0=r0):
            src = xnT[:, :].rearrange("p (r c) -> p r c", c=128)
            dst = XFT_d[:, :, r0:r0 + 128].rearrange("r p c -> p r c")
            return nc.gpsimd.dma_start(out=dst, in_=src)
        sp.d("gpsimd", store_xft, 1)
        def r_pe():
            return [nc.tensor.matmul(ps_m[:, 0:8], xnT[:, c * 128:(c + 1) * 128], rw[:, c * 8:(c + 1) * 8],
                                     start=(c == 0), stop=(c == 7)) for c in range(8)]
        sp.c("tensor", r_pe)
        sp.c("scalar", lambda: nc.scalar.copy(lg[:, :], ps_m[:, 0:8]))
        sp.c("vector", lambda: nc.vector.reduce_max(t['mean'][:, 0:1], lg[:, :], axis=AX.X))
        sp.c("scalar", lambda: nc.scalar.activation(sig[:, 0:1], t['mean'][:, 0:1], AF.Sigmoid, scale=1.0))
        sp.c("vector", lambda: nc.vector.tensor_scalar(wcols[:, :], lg[:, :], t['mean'][:, 0:1], None, ALU.is_equal))
        sp.c("vector", lambda: nc.vector.tensor_scalar(wcols[:, :], wcols[:, :], sig[:, 0:1], None, ALU.mult))
        sp.c("tensor", lambda: nc.tensor.matmul(ps_m[0:8, 0:128], wcols[:, 0:8], identf[:, :], start=True, stop=True))
        sp.c("scalar", lambda: nc.scalar.copy(r8[:, :], ps_m[0:8, 0:128]))
        sp.d("gpsimd", lambda r0=r0: nc.gpsimd.dma_start(out=wT_d[:, r0:r0 + 128], in_=r8[:, :]), 1)

    sp.c("vector", lambda: nc.vector.memset(acc[:, :], 0.0))

    def expert(e_idx, g_dram, u_dram, d_dram, use_w):
        for c in range(8):
            sp.d("sync", lambda c=c, g_dram=g_dram: nc.sync.dma_start(out=wtmp[:, :], in_=g_dram[c * 128:(c + 1) * 128, :]), 1)
            sp.c("vector", lambda c=c: nc.vector.tensor_copy(g_b[:, c * 2048:(c + 1) * 2048], wtmp[:, :]))
        for c in range(8):
            sp.d("sync", lambda c=c, u_dram=u_dram: nc.sync.dma_start(out=wtmp[:, :], in_=u_dram[c * 128:(c + 1) * 128, :]), 1)
            sp.c("vector", lambda c=c: nc.vector.tensor_copy(u_b[:, c * 2048:(c + 1) * 2048], wtmp[:, :]))
        for c in range(16):
            sp.d("sync", lambda c=c, d_dram=d_dram: nc.sync.dma_start(out=wtmp[:, 0:1024], in_=d_dram[c * 128:(c + 1) * 128, :]), 1)
            sp.c("vector", lambda c=c: nc.vector.tensor_copy(d_b[:, c * 1024:(c + 1) * 1024], wtmp[:, 0:1024]))
        for g in range(4):
            g0 = g * 512

            def load_xft(g0=g0):
                src = XFT_d[:, :, g0:g0 + 512].rearrange("r p c -> p r c")
                dst = XFTg[:, :].rearrange("p (r c) -> p r c", c=512)
                return nc.sync.dma_start(out=dst, in_=src)
            sp.d("sync", load_xft, 1)
            if use_w:
                sp.d("sync", lambda g0=g0, e_idx=e_idx: nc.sync.dma_start(out=wrow[:, :], in_=wT_d[e_idx:e_idx + 1, g0:g0 + 512]), 1)
                sp.c("tensor", lambda: nc.tensor.matmul(ps_m[:, :], ones_f[0:1, :], wrow[0:1, :], start=True, stop=True))
                sp.c("scalar", lambda: nc.scalar.copy(wB[:, :], ps_m[:, :]))
            for i in range(16):
                def zg_pe(i=i):
                    return [nc.tensor.matmul(ps0[:, :],
                                             g_b[:, kc * 2048 + i * 128: kc * 2048 + i * 128 + 128],
                                             XFTg[:, kc * 512:(kc + 1) * 512],
                                             start=(kc == 0), stop=(kc == 7)) for kc in range(8)]
                sp.c("tensor", zg_pe)

                def zu_pe(i=i):
                    return [nc.tensor.matmul(ps1[:, :],
                                             u_b[:, kc * 2048 + i * 128: kc * 2048 + i * 128 + 128],
                                             XFTg[:, kc * 512:(kc + 1) * 512],
                                             start=(kc == 0), stop=(kc == 7)) for kc in range(8)]
                sp.c("tensor", zu_pe)
                if use_w:
                    sp.c("vector", lambda: nc.vector.tensor_mul(sb_a[:, :], ps0[:, :], wB[:, :]))
                    sp.c("scalar", lambda: nc.scalar.activation(sb_b[:, :], sb_a[:, :], AF.Silu, scale=1.0))
                    sp.c("vector", lambda: nc.vector.tensor_mul(sb_a[:, :], ps1[:, :], wB[:, :]))
                    sp.c("vector", lambda i=i: nc.vector.tensor_mul(hm[:, i * 512:(i + 1) * 512], sb_b[:, :], sb_a[:, :]))
                else:
                    sp.c("scalar", lambda: nc.scalar.activation(sb_b[:, :], ps0[:, :], AF.Silu, scale=1.0))
                    sp.c("vector", lambda i=i: nc.vector.tensor_mul(hm[:, i * 512:(i + 1) * 512], sb_b[:, :], ps1[:, :]))
            for s4 in range(4):
                for nh in range(2):
                    def dn_pe(s4=s4, nh=nh):
                        return [nc.tensor.matmul((ps2 if nh == 0 else ps3)[:, :],
                                                 hm[:, i * 512 + s4 * 128: i * 512 + s4 * 128 + 128],
                                                 d_b[:, i * 1024 + nh * 512: i * 1024 + nh * 512 + 512],
                                                 start=(i == 0), stop=(i == 15)) for i in range(16)]
                    sp.c("tensor", dn_pe)
                for nh in range(2):
                    sp.c("scalar", lambda nh=nh: nc.scalar.copy(sb_c[:, :], (ps2 if nh == 0 else ps3)[:, :]))

                    def accadd(g0=g0, s4=s4, nh=nh):
                        col = ((g0 + s4 * 128) // 128) * 1024 + nh * 512
                        return nc.vector.tensor_add(acc[:, col:col + 512], acc[:, col:col + 512], sb_c[:, :])
                    sp.c("vector", accadd)

    for e in range(8):
        expert(e, gate_w[e, :, :], up_w[e, :, :], down_w[e, :, :], True)
    expert(0, sh_gate, sh_up, sh_down, False)

    for tile in range(16):
        r0 = tile * 128
        sp.d("sync", lambda r0=r0: nc.sync.dma_start(out=hT[:, :], in_=h_d[r0:r0 + 128, :]), 1)
        sp.c("vector", lambda tile=tile: nc.vector.tensor_copy(xf[:, :], acc[:, tile * 1024:(tile + 1) * 1024]))
        sp.c("vector", lambda: nc.vector.tensor_add(xf[:, :], xf[:, :], hT[:, :]))
        ln_plain(sp, nc, xf[:, :], xf[:, :], D, t)
        sp.d("sync", lambda r0=r0: nc.sync.dma_start(out=xt[:, :], in_=x1[r0:r0 + 128, :]), 1)
        sp.c("vector", lambda: nc.vector.tensor_add(xf[:, :], xf[:, :], xt[:, :]))
        sp.d("gpsimd", lambda r0=r0: nc.gpsimd.dma_start(out=xo[r0:r0 + 128, :], in_=xf[:, :]), 1)

    sp.build()
    st.close()
    return nc


def build_l1_attn():
    nc = bass.Bass("TRN2", target_bir_lowering=False)
    xb = nc.dram_tensor("xb", [T, D], F32, kind="ExternalInput").ap()
    xq = nc.dram_tensor("xq", [2048, D], F32, kind="ExternalInput").ap()
    Wqd = nc.dram_tensor("Wqd", [D, 256], F32, kind="ExternalInput").ap()
    Wkd = nc.dram_tensor("Wkd", [D, 256], F32, kind="ExternalInput").ap()
    WvdT = nc.dram_tensor("WvdT", [256, D], F32, kind="ExternalInput").ap()
    Wvu = nc.dram_tensor("Wvu", [256, D], F32, kind="ExternalInput").ap()
    iotad = nc.dram_tensor("iotad", [1, T], F32, kind="ExternalInput").ap()
    qidxd = nc.dram_tensor("qidxd", [128, 16], F32, kind="ExternalInput").ap()
    onesd = nc.dram_tensor("onesd", [1, 128], F32, kind="ExternalInput").ap()
    identd = nc.dram_tensor("identd", [128, 128], BF16, kind="ExternalInput").ap()
    xo = nc.dram_tensor("xo", [2048, D], F32, kind="ExternalOutput").ap()

    QT_d = nc.dram_tensor("QT_d", [2, 128, 2048], BF16, kind="Internal").ap()
    KT_d = nc.dram_tensor("KT_d", [2, 128, T], BF16, kind="Internal").ap()
    V_d = nc.dram_tensor("V_d", [T, D], BF16, kind="Internal").ap()

    sp = SP(nc)
    st = contextlib.ExitStack()
    sb = lambda n, s, dt=F32: st.enter_context(nc.sbuf_tensor(n, s, dt))
    pst = lambda n, s, dt=F32: st.enter_context(nc.psum_tensor(n, s, dt))

    ones_f = sb("ones_f", [1, 128])
    identb = sb("identb", [128, 128], BF16)
    epsc = sb("epsc", [128, 1])
    wq_b = sb("wq_b", [128, 8 * 256], BF16)
    wk_b = sb("wk_b", [128, 8 * 256], BF16)
    wvdT_b = sb("wvdT_b", [128, 2 * 1024], BF16)
    wvu_b = sb("wvu_b", [128, 2 * 1024], BF16)
    M_b = sb("M_b", [128, 8 * 1024], BF16)
    wtmp = sb("wtmp", [128, 1024])
    iob = sb("iob", [128, T])
    qix = sb("qix", [128, 16])
    msk5 = sb("msk5", [128, 512])
    tmp5 = sb("tmp5", [128, 512])
    xt = sb("xt", [128, D])
    xn = sb("xn", [128, D])
    xnb = sb("xnb", [128, D], BF16)
    xnT = sb("xnT", [128, 8 * 128], BF16)
    stage = sb("stage", [128, 1280], BF16)
    KT_sb = sb("KT_sb", [128, 2 * T], BF16)
    V_sb = sb("V_sb", [128, 32 * 1024], BF16)
    qtT = sb("qtT", [128, 2 * 128], BF16)
    S_f = sb("S_f", [128, T])
    A_bf = sb("A_bf", [128, T], BF16)
    ATb = sb("ATb", [128, 32 * 128], BF16)
    yo = sb("yo", [128, D])
    t = dict(mean=sb("mean", [128, 1]), var=sb("var", [128, 1]), rstd=sb("rstd", [128, 1]),
             s2=sb("s2", [128, 1]), xc=sb("xc", [128, D]), sq=sb("sq", [128, D]), epsc=epsc)

    ps0 = pst("ps0", [128, 512])
    ps1 = pst("ps1", [128, 512])
    ps2 = pst("ps2", [128, 512])
    ps3 = pst("ps3", [128, 512])
    ps_m = pst("ps_m", [128, 512])
    ps_tb = pst("ps_tb", [128, 512], BF16)
    tbk = {'identb': identb, 'ps': ps_tb}
    pslist = [ps0, ps1, ps2, ps3]

    sp.d("sync", lambda: [nc.sync.dma_start(out=ones_f[:, :], in_=onesd[:, :]),
                          nc.sync.dma_start(out=identb[:, :], in_=identd[:, :])], 2)
    sp.c("vector", lambda: nc.vector.memset(epsc[:, :], EPS))
    for kg in range(8):
        sp.d("sync", lambda kg=kg: nc.sync.dma_start(out=wtmp[0:1, 0:512], in_=iotad[0:1, kg * 512:(kg + 1) * 512]), 1)
        bcast_row(sp, nc, iob[:, kg * 512:(kg + 1) * 512], wtmp, 512, ones_f, ps_m)

    sp.d("sync", lambda: nc.sync.dma_start(out=qix[:, :], in_=qidxd[:, :]), 1)

    load_w_bf16(sp, nc, wq_b, Wqd, D, 256, wtmp, 0)
    load_w_bf16(sp, nc, wk_b, Wkd, D, 256, wtmp, 0)
    load_w_bf16(sp, nc, wvdT_b, WvdT, 256, D, wtmp, 0)
    load_w_bf16(sp, nc, wvu_b, Wvu, 256, D, wtmp, 0)
    for it in range(8):
        for nh in range(2):
            def pe(it=it, nh=nh):
                return [nc.tensor.matmul(ps0[:, :],
                                         wvdT_b[:, c * 1024 + it * 128: c * 1024 + it * 128 + 128],
                                         wvu_b[:, c * 1024 + nh * 512: c * 1024 + nh * 512 + 512],
                                         start=(c == 0), stop=(c == 1)) for c in range(2)]
            sp.c("tensor", pe)
            sp.c("scalar", lambda it=it, nh=nh: nc.scalar.copy(M_b[:, it * 1024 + nh * 512: it * 1024 + nh * 512 + 512], ps0[:, :]))

    def ln_tile(x_src, tile):
        r0 = tile * 128
        sp.d("sync", lambda: nc.sync.dma_start(out=xt[:, :], in_=x_src[r0:r0 + 128, :]), 1)
        ln_plain(sp, nc, xn[:, :], xt[:, :], D, t)
        sp.c("vector", lambda: nc.vector.tensor_copy(xnb[:, :], xn[:, :]))
        transpose_to(sp, nc, lambda j: xnT[:, j * 128:(j + 1) * 128],
                     lambda j: xnb[:, j * 128:(j + 1) * 128], 8, tbk)

    for tile in range(32):
        ln_tile(xb, tile)
        for ck in range(2):
            def pe(ck=ck):
                return [nc.tensor.matmul(ps0[:, 0:128],
                                         wk_b[:, c * 256 + ck * 128: c * 256 + ck * 128 + 128],
                                         xnT[:, c * 128:(c + 1) * 128],
                                         start=(c == 0), stop=(c == 7)) for c in range(8)]
            sp.c("tensor", pe)
            sp.c("scalar", lambda ck=ck: nc.scalar.copy(stage[:, ck * 128:(ck + 1) * 128], ps0[:, 0:128]))

        def store_kt(tile=tile):
            src = stage[:, 0:256].rearrange("p (r c) -> p r c", c=128)
            dst = KT_d[:, :, tile * 128:tile * 128 + 128].rearrange("r p c -> p r c")
            return nc.gpsimd.dma_start(out=dst, in_=src)
        sp.d("gpsimd", store_kt, 1)
        for nh in range(2):
            def pe(nh=nh):
                return [nc.tensor.matmul((ps1 if nh == 0 else ps2)[:, :],
                                         xnT[:, c * 128:(c + 1) * 128],
                                         M_b[:, c * 1024 + nh * 512: c * 1024 + nh * 512 + 512],
                                         start=(c == 0), stop=(c == 7)) for c in range(8)]
            sp.c("tensor", pe)
            sp.c("scalar", lambda nh=nh: nc.scalar.copy(stage[:, 256 + nh * 512: 256 + nh * 512 + 512], (ps1 if nh == 0 else ps2)[:, :]))
        sp.d("gpsimd", lambda tile=tile: nc.gpsimd.dma_start(out=V_d[tile * 128:tile * 128 + 128, :], in_=stage[:, 256:256 + 1024]), 1)

    for tile in range(16):
        ln_tile(xq, tile)
        for ck in range(2):
            def pe(ck=ck):
                return [nc.tensor.matmul(ps0[:, 0:128],
                                         wq_b[:, c * 256 + ck * 128: c * 256 + ck * 128 + 128],
                                         xnT[:, c * 128:(c + 1) * 128],
                                         start=(c == 0), stop=(c == 7)) for c in range(8)]
            sp.c("tensor", pe)
            sp.c("scalar", lambda ck=ck: nc.scalar.mul(stage[:, ck * 128:(ck + 1) * 128], ps0[:, 0:128], 1.0 / 16.0))

        def store_qt(tile=tile):
            src = stage[:, 0:256].rearrange("p (r c) -> p r c", c=128)
            dst = QT_d[:, :, tile * 128:tile * 128 + 128].rearrange("r p c -> p r c")
            return nc.gpsimd.dma_start(out=dst, in_=src)
        sp.d("gpsimd", store_qt, 1)

    def load_ktsb():
        src = KT_d[:, :, :].rearrange("r p c -> p r c")
        dst = KT_sb[:, :].rearrange("p (r c) -> p r c", c=T)
        return nc.sync.dma_start(out=dst, in_=src)
    sp.d("sync", load_ktsb, 1)

    def load_vsb():
        src = V_d[:, :].rearrange("(tt q) c -> q tt c", q=128)
        dst = V_sb[:, :].rearrange("q (tt c) -> q tt c", c=1024)
        return nc.sync.dma_start(out=dst, in_=src)
    sp.d("sync", load_vsb, 1)

    for qt in range(16):
        def load_qtT(qt=qt):
            src = QT_d[:, :, qt * 128:qt * 128 + 128].rearrange("r p c -> p r c")
            dst = qtT[:, :].rearrange("p (r c) -> p r c", c=128)
            return nc.sync.dma_start(out=dst, in_=src)
        sp.d("sync", load_qtT, 1)
        for kh in range(2):
            def pe(kh=kh):
                out = []
                for j in range(4):
                    kg = kh * 4 + j
                    for c in range(2):
                        out.append(nc.tensor.matmul(pslist[j][:, :],
                                                    qtT[:, c * 128:(c + 1) * 128],
                                                    KT_sb[:, c * T + kg * 512: c * T + kg * 512 + 512],
                                                    start=(c == 0), stop=(c == 1)))
                return out
            sp.c("tensor", pe)

            def ev(kh=kh):
                return [nc.scalar.copy(S_f[:, (kh * 4 + j) * 512:(kh * 4 + j) * 512 + 512], pslist[j][:, :]) for j in range(4)]
            sp.c("scalar", ev)
        for kg in range(8):
            ks = kg * 512
            sp.c("vector", lambda qt=qt, ks=ks: nc.vector.tensor_scalar(msk5[:, :], iob[:, ks:ks + 512], qix[:, qt:qt + 1], None, ALU.is_le))
            sp.c("vector", lambda ks=ks: nc.vector.tensor_mul(S_f[:, ks:ks + 512], S_f[:, ks:ks + 512], msk5[:, :]))
            sp.c("vector", lambda: nc.vector.tensor_scalar(tmp5[:, :], msk5[:, :], 1e38, -1e38, ALU.mult, ALU.add))
            sp.c("vector", lambda ks=ks: nc.vector.tensor_add(S_f[:, ks:ks + 512], S_f[:, ks:ks + 512], tmp5[:, :]))
        softmax_rows(sp, nc, A_bf[:, :], S_f[:, :], t)
        transpose_to(sp, nc, lambda j: ATb[:, j * 128:(j + 1) * 128],
                     lambda j: A_bf[:, j * 128:(j + 1) * 128], 32, tbk)
        for nh in range(2):
            def pe(nh=nh):
                return [nc.tensor.matmul((ps0 if nh == 0 else ps1)[:, :],
                                         ATb[:, kt * 128:(kt + 1) * 128],
                                         V_sb[:, kt * 1024 + nh * 512: kt * 1024 + nh * 512 + 512],
                                         start=(kt == 0), stop=(kt == 31)) for kt in range(32)]
            sp.c("tensor", pe)
            sp.c("scalar", lambda nh=nh: nc.scalar.copy(yo[:, nh * 512:nh * 512 + 512], (ps0 if nh == 0 else ps1)[:, :]))
        ln_plain(sp, nc, yo[:, :], yo[:, :], D, t)
        sp.d("sync", lambda qt=qt: nc.sync.dma_start(out=xt[:, :], in_=xq[qt * 128:qt * 128 + 128, :]), 1)
        sp.c("vector", lambda: nc.vector.tensor_add(yo[:, :], yo[:, :], xt[:, :]))
        sp.d("gpsimd", lambda qt=qt: nc.gpsimd.dma_start(out=xo[qt * 128:qt * 128 + 128, :], in_=yo[:, :]), 1)

    sp.build()
    st.close()
    return nc


def build_lnpre():
    nc = bass.Bass("TRN2", target_bir_lowering=False)
    NT = 2048
    xin = nc.dram_tensor("xin", [NT, D], F32, kind="ExternalInput").ap()
    hout = nc.dram_tensor("hout", [NT, D], F32, kind="ExternalOutput").ap()
    sp = SP(nc)
    st = contextlib.ExitStack()
    sb = lambda n, s, dt=F32: st.enter_context(nc.sbuf_tensor(n, s, dt))
    xt = sb("xt", [128, D])
    hn = sb("hn", [128, D])
    epsc = sb("epsc", [128, 1])
    t = dict(mean=sb("mean", [128, 1]), var=sb("var", [128, 1]), rstd=sb("rstd", [128, 1]),
             s2=sb("s2", [128, 1]), xc=sb("xc", [128, D]), sq=sb("sq", [128, D]), epsc=epsc)
    sp.c("vector", lambda: nc.vector.memset(epsc[:, :], EPS))
    for tile in range(16):
        r0 = tile * 128
        sp.d("sync", lambda r0=r0: nc.sync.dma_start(out=xt[:, :], in_=xin[r0:r0 + 128, :]), 1)
        ln_plain(sp, nc, hn[:, :], xt[:, :], D, t)
        sp.d("gpsimd", lambda r0=r0: nc.gpsimd.dma_start(out=hout[r0:r0 + 128, :], in_=hn[:, :]), 1)
    sp.build()
    st.close()
    return nc


def build_lnpost():
    nc = bass.Bass("TRN2", target_bir_lowering=False)
    NT = 2048
    x3in = nc.dram_tensor("x3in", [NT, D], F32, kind="ExternalInput").ap()
    hin = nc.dram_tensor("hin", [NT, D], F32, kind="ExternalInput").ap()
    xout = nc.dram_tensor("xout", [NT, D], F32, kind="ExternalOutput").ap()
    sp = SP(nc)
    st = contextlib.ExitStack()
    sb = lambda n, s, dt=F32: st.enter_context(nc.sbuf_tensor(n, s, dt))
    xt = sb("xt", [128, D])
    hn = sb("hn", [128, D])
    epsc = sb("epsc", [128, 1])
    t = dict(mean=sb("mean", [128, 1]), var=sb("var", [128, 1]), rstd=sb("rstd", [128, 1]),
             s2=sb("s2", [128, 1]), xc=sb("xc", [128, D]), sq=sb("sq", [128, D]), epsc=epsc)
    sp.c("vector", lambda: nc.vector.memset(epsc[:, :], EPS))
    for tile in range(16):
        r0 = tile * 128
        sp.d("sync", lambda r0=r0: nc.sync.dma_start(out=xt[:, :], in_=hin[r0:r0 + 128, :]), 1)
        ln_plain(sp, nc, hn[:, :], xt[:, :], D, t)
        sp.d("sync", lambda r0=r0: nc.sync.dma_start(out=xt[:, :], in_=x3in[r0:r0 + 128, :]), 1)
        sp.c("vector", lambda: nc.vector.tensor_add(hn[:, :], hn[:, :], xt[:, :]))
        sp.d("gpsimd", lambda r0=r0: nc.gpsimd.dma_start(out=xout[r0:r0 + 128, :], in_=hn[:, :]), 1)
    sp.build()
    st.close()
    return nc


NB = 2816
NTILE = 22
QT0, QT1 = 5, 22
FT0, FT1 = 6, 22
DI = 2730


def build_l1_inner():
    nc = bass.Bass("TRN2", target_bir_lowering=False)
    xin = nc.dram_tensor("xin", [NB, D], F32, kind="ExternalInput").ap()
    Wqkv = nc.dram_tensor("Wqkv", [D, 3072], F32, kind="ExternalInput").ap()
    Wod = nc.dram_tensor("Wod", [D, D], F32, kind="ExternalInput").ap()
    W1d = nc.dram_tensor("W1d", [D, 2 * DI], F32, kind="ExternalInput").ap()
    b1d = nc.dram_tensor("b1d", [1, 2 * DI], F32, kind="ExternalInput").ap()
    W2d = nc.dram_tensor("W2d", [DI, D], F32, kind="ExternalInput").ap()
    b2d = nc.dram_tensor("b2d", [1, D], F32, kind="ExternalInput").ap()
    lngd = nc.dram_tensor("lngd", [1, D], F32, kind="ExternalInput").ap()
    lnbd = nc.dram_tensor("lnbd", [1, D], F32, kind="ExternalInput").ap()
    qscd = nc.dram_tensor("qscd", [1, 64], F32, kind="ExternalInput").ap()
    kscd = nc.dram_tensor("kscd", [1, 64], F32, kind="ExternalInput").ap()
    tabA = nc.dram_tensor("tabA", [NB, 64], F32, kind="ExternalInput").ap()
    tabB = nc.dram_tensor("tabB", [NB, 64], F32, kind="ExternalInput").ap()
    tabAi = nc.dram_tensor("tabAi", [NB, 64], F32, kind="ExternalInput").ap()
    tabBi = nc.dram_tensor("tabBi", [NB, 64], F32, kind="ExternalInput").ap()
    iota5d = nc.dram_tensor("iota5d", [1, 512], F32, kind="ExternalInput").ap()
    ilcd = nc.dram_tensor("ilcd", [128, 2], F32, kind="ExternalInput").ap()
    ilc2d = nc.dram_tensor("ilc2d", [128, 2], F32, kind="ExternalInput").ap()
    thrd = nc.dram_tensor("thrd", [128, 17], F32, kind="ExternalInput").ap()
    onesd = nc.dram_tensor("onesd", [1, 128], F32, kind="ExternalInput").ap()
    onesbd = nc.dram_tensor("onesbd", [1, 128], BF16, kind="ExternalInput").ap()
    identd = nc.dram_tensor("identd", [128, 128], BF16, kind="ExternalInput").ap()
    xout = nc.dram_tensor("xout", [2048, D], F32, kind="ExternalOutput").ap()

    QT_dd = nc.dram_tensor("QT_dd", [8, 128, NB], BF16, kind="Internal").ap()
    KT_dd = nc.dram_tensor("KT_dd", [8, 128, NB], BF16, kind="Internal").ap()
    V_dd = nc.dram_tensor("V_dd", [NB, D], BF16, kind="Internal").ap()
    y_d = nc.dram_tensor("y_d", [NB, D], F32, kind="Internal").ap()

    sp = SP(nc)
    st = contextlib.ExitStack()
    sb = lambda n, s, dt=F32: st.enter_context(nc.sbuf_tensor(n, s, dt))
    pst = lambda n, s, dt=F32: st.enter_context(nc.psum_tensor(n, s, dt))

    ones_f = sb("ones_f", [1, 128])
    ones_b = sb("ones_b", [1, 128], BF16)
    identb = sb("identb", [128, 128], BF16)
    epsc = sb("epsc", [128, 1])
    arena = sb("arena", [128, 43680], BF16)
    WO_OFF = 24576
    w2s = sb("w2s", [128, 1024], BF16)
    wtmp = sb("wtmp", [128, 2048])
    b1row = sb("b1row", [1, 2 * DI], BF16)
    b2row = sb("b2row", [1, D], BF16)
    lngB = sb("lngB", [128, D])
    lnbB = sb("lnbB", [128, D])
    qsb = sb("qsb", [128, 64])
    ksb = sb("ksb", [128, 64])
    io5 = sb("io5", [128, 512])
    ilc = sb("ilc", [128, 2])
    ilc2 = sb("ilc2", [128, 2])
    thr = sb("thr", [128, 17])
    tA = sb("tA", [128, 64])
    tB = sb("tB", [128, 64])
    xt = sb("xt", [128, D])
    xsb = sb("xsb", [128, D], BF16)
    xnT = sb("xnT", [128, 8 * 128], BF16)
    qf = sb("qf", [128, D])
    kf = sb("kf", [128, D])
    rot = sb("rot", [128, D])
    rtmp = sb("rtmp", [128, D])
    red = sb("red", [128, 16])
    stage = sb("stage", [128, 1024], BF16)
    qtT8 = sb("qtT8", [128, 8 * 128], BF16)
    kt512 = sb("kt512", [128, 8 * 512], BF16)
    V4 = sb("V4", [128, 4 * 1024], BF16)
    msk5 = sb("msk5", [128, 512])
    tmp5 = sb("tmp5", [128, 512])
    addm5 = sb("addm5", [128, 512])
    S5 = sb("S5", [128, 512])
    A5 = sb("A5", [128, 512], BF16)
    AT5 = sb("AT5", [128, 512], BF16)
    YT8 = sb("YT8", [128, 8 * 128], BF16)
    yo = sb("yo", [128, D])
    sb_g = sb("sb_g", [128, 512])
    mid_bf = sb("mid_bf", [128, DI], BF16)
    midT = sb("midT", [128, 22 * 128], BF16)
    t = dict(mean=sb("mean", [128, 1]), var=sb("var", [128, 1]), rstd=sb("rstd", [128, 1]),
             s2=sb("s2", [128, 1]), xc=sb("xc", [128, D]), sq=sb("sq", [128, D]), epsc=epsc)

    ps0 = pst("ps0", [128, 512])
    ps1 = pst("ps1", [128, 512])
    ps_y = pst("ps_y", [128, 512])
    ps_m = pst("ps_m", [128, 512])
    ps_tb = pst("ps_tb", [128, 512], BF16)
    tbk = {'identb': identb, 'ps': ps_tb}

    sp.d("sync", lambda: [nc.sync.dma_start(out=ones_f[:, :], in_=onesd[:, :]),
                          nc.sync.dma_start(out=ones_b[:, :], in_=onesbd[:, :]),
                          nc.sync.dma_start(out=identb[:, :], in_=identd[:, :]),
                          nc.sync.dma_start(out=ilc[:, :], in_=ilcd[:, :]),
                          nc.sync.dma_start(out=ilc2[:, :], in_=ilc2d[:, :]),
                          nc.sync.dma_start(out=thr[:, :], in_=thrd[:, :])], 6)
    sp.c("vector", lambda: nc.vector.memset(epsc[:, :], EPS))
    sp.d("sync", lambda: nc.sync.dma_start(out=wtmp[0:1, 0:512], in_=iota5d[:, :]), 1)
    bcast_row(sp, nc, io5[:, :], wtmp, 512, ones_f, ps_m)
    sp.d("sync", lambda: nc.sync.dma_start(out=wtmp[0:1, 0:64], in_=qscd[:, :]), 1)
    bcast_row(sp, nc, qsb[:, :], wtmp, 64, ones_f, ps_m)
    sp.d("sync", lambda: nc.sync.dma_start(out=wtmp[0:1, 0:64], in_=kscd[:, :]), 1)
    bcast_row(sp, nc, ksb[:, :], wtmp, 64, ones_f, ps_m)
    sp.d("sync", lambda: nc.sync.dma_start(out=wtmp[0:1, 0:D], in_=lngd[:, :]), 1)
    bcast_row(sp, nc, lngB[:, :], wtmp, D, ones_f, ps_m)
    sp.d("sync", lambda: nc.sync.dma_start(out=wtmp[0:1, 0:D], in_=lnbd[:, :]), 1)
    bcast_row(sp, nc, lnbB[:, :], wtmp, D, ones_f, ps_m)
    for hf in range(3):
        n0, n1 = hf * 1820, min((hf + 1) * 1820, 2 * DI)
        sp.d("sync", lambda n0=n0, n1=n1: nc.sync.dma_start(out=wtmp[0:1, 0:n1 - n0], in_=b1d[0:1, n0:n1]), 1)
        sp.c("vector", lambda n0=n0, n1=n1: nc.vector.tensor_copy(b1row[0:1, n0:n1], wtmp[0:1, 0:n1 - n0]))
    sp.d("sync", lambda: nc.sync.dma_start(out=wtmp[0:1, 0:D], in_=b2d[:, :]), 1)
    sp.c("vector", lambda: nc.vector.tensor_copy(b2row[:, :], wtmp[0:1, 0:D]))

    load_w_bf16(sp, nc, arena, Wqkv, D, 3072, wtmp, 0, nsplit=2)
    load_w_bf16(sp, nc, arena, Wod, D, D, wtmp, WO_OFF)

    for tile in range(NTILE):
        r0 = tile * 128
        rm = max(r0 - 1, 0)

        def load_shift(r0=r0, rm=rm):
            return [nc.sync.dma_start(out=xt[:, 0:512], in_=xin[r0:r0 + 128, 0:512]),
                    nc.sync.dma_start(out=xt[:, 512:1024], in_=xin[rm:rm + 128, 512:1024])]
        sp.d("sync", load_shift, 2)
        sp.c("vector", lambda: nc.vector.tensor_copy(xsb[:, :], xt[:, :]))
        transpose_to(sp, nc, lambda j: xnT[:, j * 128:(j + 1) * 128],
                     lambda j: xsb[:, j * 128:(j + 1) * 128], 8, tbk)
        for grp in range(6):
            def pe(grp=grp):
                return [nc.tensor.matmul((ps0 if grp % 2 == 0 else ps1)[:, :],
                                         xnT[:, c * 128:(c + 1) * 128],
                                         arena[:, c * 3072 + grp * 512: c * 3072 + grp * 512 + 512],
                                         start=(c == 0), stop=(c == 7)) for c in range(8)]
            sp.c("tensor", pe)
            if grp < 2:
                sp.c("scalar", lambda grp=grp: nc.scalar.copy(qf[:, grp * 512:grp * 512 + 512], (ps0 if grp % 2 == 0 else ps1)[:, :]))
            elif grp < 4:
                sp.c("scalar", lambda grp=grp: nc.scalar.copy(kf[:, (grp - 2) * 512:(grp - 2) * 512 + 512], (ps0 if grp % 2 == 0 else ps1)[:, :]))
            else:
                sp.c("scalar", lambda grp=grp: nc.scalar.copy(stage[:, (grp - 4) * 512:(grp - 4) * 512 + 512], (ps0 if grp % 2 == 0 else ps1)[:, :]))
        sp.d("gpsimd", lambda r0=r0: nc.gpsimd.dma_start(out=V_dd[r0:r0 + 128, :], in_=stage[:, :]), 1)

        def qk_process(qk, scb, tA_d, tB_d, out_d, r0=r0):
            sp.c("vector", lambda: nc.vector.tensor_mul(t['xc'][:, :], qk[:, :], qk[:, :]))

            def red_f():
                xv = t['xc'][:, :].rearrange("p (h j) -> p h j", j=64)
                return nc.vector.reduce_sum(red[:, :].rearrange("p (h j) -> p h j", j=1), xv, axis=AX.X)
            sp.c("vector", red_f)
            sp.c("scalar", lambda: nc.scalar.activation(red[:, :], red[:, :], AF.Sqrt, scale=1.0))
            sp.c("vector", lambda: nc.vector.tensor_single_scalar(red[:, :], red[:, :], 1e-12, ALU.max))
            sp.c("vector", lambda: nc.vector.reciprocal(red[:, :], red[:, :]))

            def nrm():
                qv = qk[:, :].rearrange("p (h j) -> p h j", j=64)
                a, b = broadcast_tensor_aps(qv, red[:, :].rearrange("p (h j) -> p h j", j=1))
                return nc.vector.tensor_tensor(qv, a, b, ALU.mult)
            sp.c("vector", nrm)
            head_bcast_mul(sp, nc, qk[:, :], scb, 16, 64)
            sp.d("sync", lambda: [nc.sync.dma_start(out=tA[:, :], in_=tA_d[r0:r0 + 128, :]),
                                  nc.sync.dma_start(out=tB[:, :], in_=tB_d[r0:r0 + 128, :])], 2)
            rope_apply(sp, nc, qk[:, :], rot, tA, tB, 16, 64, rtmp)
            sp.c("vector", lambda: nc.vector.tensor_copy(xsb[:, :], qk[:, :]))
            transpose_to(sp, nc, lambda j: stage[:, j * 128:(j + 1) * 128],
                         lambda j: xsb[:, j * 128:(j + 1) * 128], 8, tbk)

            def store(r0=r0):
                src = stage[:, 0:1024].rearrange("p (r c) -> p r c", c=128)
                dst = out_d[:, :, r0:r0 + 128].rearrange("r p c -> p r c")
                return nc.gpsimd.dma_start(out=dst, in_=src)
            sp.d("gpsimd", store, 1)

        qk_process(qf, qsb, tabA, tabB, QT_dd)
        qk_process(kf, ksb, tabAi, tabBi, KT_dd)

    for qt in range(QT0, QT1):
        w = qt // 2
        k0 = (w - 1) * 256
        parity = qt % 2

        def ld_q(qt=qt):
            src = QT_dd[:, :, qt * 128:qt * 128 + 128].rearrange("r p c -> p r c")
            dst = qtT8[:, :].rearrange("p (r c) -> p r c", c=128)
            return nc.sync.dma_start(out=dst, in_=src)
        sp.d("sync", ld_q, 1)

        def ld_k(k0=k0):
            src = KT_dd[:, :, k0:k0 + 512].rearrange("r p c -> p r c")
            dst = kt512[:, :].rearrange("p (r c) -> p r c", c=512)
            return nc.sync.dma_start(out=dst, in_=src)
        sp.d("sync", ld_k, 1)

        def ld_v(k0=k0):
            src = V_dd[k0:k0 + 512, :].rearrange("(kt p) c -> p kt c", p=128)
            dst = V4[:, :].rearrange("p (kt c) -> p kt c", c=1024)
            return nc.sync.dma_start(out=dst, in_=src)
        sp.d("sync", ld_v, 1)
        sp.c("vector", lambda parity=parity: nc.vector.tensor_scalar(msk5[:, :], io5[:, :], ilc[:, parity:parity + 1], None, ALU.is_ge))
        sp.c("vector", lambda parity=parity: nc.vector.tensor_scalar(tmp5[:, :], io5[:, :], ilc2[:, parity:parity + 1], None, ALU.is_le))
        sp.c("vector", lambda: nc.vector.tensor_mul(msk5[:, :], msk5[:, :], tmp5[:, :]))
        sp.c("vector", lambda qt=qt: nc.vector.tensor_scalar(tmp5[:, :], io5[:, :], thr[:, qt - QT0:qt - QT0 + 1], None, ALU.is_ge))
        sp.c("vector", lambda: nc.vector.tensor_mul(msk5[:, :], msk5[:, :], tmp5[:, :]))
        sp.c("vector", lambda: nc.vector.tensor_scalar(addm5[:, :], msk5[:, :], 1e38, -1e38, ALU.mult, ALU.add))
        for h in range(16):
            hp, hh = h // 2, h % 2
            hb = hh * 64

            def pe_s(hp=hp, hb=hb):
                return nc.tensor.matmul(ps0[:, :], qtT8[hb:hb + 64, hp * 128:hp * 128 + 128],
                                        kt512[hb:hb + 64, hp * 512:hp * 512 + 512], start=True, stop=True)
            sp.c("tensor", pe_s)
            sp.c("vector", lambda: nc.vector.tensor_mul(S5[:, :], ps0[:, :], msk5[:, :]))
            sp.c("vector", lambda: nc.vector.tensor_add(S5[:, :], S5[:, :], addm5[:, :]))
            softmax_rows(sp, nc, A5[:, :], S5[:, :], t)

            def pe_t():
                return [nc.tensor.transpose(ps_tb[:, i * 128:(i + 1) * 128], A5[:, i * 128:(i + 1) * 128], identb[:, :])
                        for i in range(4)]
            sp.c("tensor", pe_t)
            sp.c("vector", lambda: nc.vector.tensor_copy(AT5[:, :], ps_tb[:, :]))

            def pe_av(hb=hb, h=h):
                return [nc.tensor.matmul(ps_y[0:64, 0:128], V4[:, kt * 1024 + h * 64: kt * 1024 + h * 64 + 64],
                                         AT5[:, kt * 128:(kt + 1) * 128], start=(kt == 0), stop=(kt == 3)) for kt in range(4)]
            sp.c("tensor", pe_av)
            sp.c("scalar", lambda hb=hb, hp=hp: nc.scalar.copy(YT8[hb:hb + 64, hp * 128:hp * 128 + 128], ps_y[0:64, 0:128]))
        for nh in range(2):
            def pe(nh=nh):
                return [nc.tensor.matmul((ps0 if nh == 0 else ps1)[:, :],
                                         YT8[:, c * 128:(c + 1) * 128],
                                         arena[:, WO_OFF + c * 1024 + nh * 512: WO_OFF + c * 1024 + nh * 512 + 512],
                                         start=(c == 0), stop=(c == 7)) for c in range(8)]
            sp.c("tensor", pe)
            sp.c("scalar", lambda nh=nh: nc.scalar.copy(yo[:, nh * 512:nh * 512 + 512], (ps0 if nh == 0 else ps1)[:, :]))
        sp.d("sync", lambda qt=qt: nc.sync.dma_start(out=xt[:, :], in_=xin[qt * 128:qt * 128 + 128, :]), 1)
        sp.c("vector", lambda: nc.vector.tensor_add(yo[:, :], yo[:, :], xt[:, :]))
        sp.d("gpsimd", lambda qt=qt: nc.gpsimd.dma_start(out=y_d[qt * 128:qt * 128 + 128, :], in_=yo[:, :]), 1)

    load_w_bf16(sp, nc, arena, W1d, D, 2 * DI, wtmp, 0, nsplit=3)
    NG = [(j * 512, min((j + 1) * 512, DI)) for j in range((DI + 511) // 512)]
    for tile in range(FT0, FT1):
        r0 = tile * 128

        def load_shift(r0=r0):
            return [nc.sync.dma_start(out=xt[:, 0:512], in_=y_d[r0:r0 + 128, 0:512]),
                    nc.sync.dma_start(out=xt[:, 512:1024], in_=y_d[r0 - 1:r0 - 1 + 128, 512:1024])]
        sp.d("sync", load_shift, 2)
        ln_plain(sp, nc, t['xc'][:, :], xt[:, :], D, t)
        sp.c("vector", lambda: nc.vector.tensor_mul(t['xc'][:, :], t['xc'][:, :], lngB[:, :]))
        sp.c("vector", lambda: nc.vector.tensor_add(t['xc'][:, :], t['xc'][:, :], lnbB[:, :]))
        sp.c("vector", lambda: nc.vector.tensor_copy(xsb[:, :], t['xc'][:, :]))
        transpose_to(sp, nc, lambda j: xnT[:, j * 128:(j + 1) * 128],
                     lambda j: xsb[:, j * 128:(j + 1) * 128], 8, tbk)
        for (n0, n1) in NG:
            wdt = n1 - n0

            def pe_a(n0=n0, n1=n1, wdt=wdt):
                out = [nc.tensor.matmul(ps0[:, 0:wdt], ones_b[0:1, :], b1row[0:1, n0:n1], start=True, stop=False)]
                out += [nc.tensor.matmul(ps0[:, 0:wdt], xnT[:, c * 128:(c + 1) * 128],
                                         arena[:, c * 2 * DI + n0: c * 2 * DI + n1],
                                         start=False, stop=(c == 7)) for c in range(8)]
                return out
            sp.c("tensor", pe_a)

            def pe_g(n0=n0, n1=n1, wdt=wdt):
                out = [nc.tensor.matmul(ps1[:, 0:wdt], ones_b[0:1, :], b1row[0:1, DI + n0:DI + n1], start=True, stop=False)]
                out += [nc.tensor.matmul(ps1[:, 0:wdt], xnT[:, c * 128:(c + 1) * 128],
                                         arena[:, c * 2 * DI + DI + n0: c * 2 * DI + DI + n1],
                                         start=False, stop=(c == 7)) for c in range(8)]
                return out
            sp.c("tensor", pe_g)
            sp.c("scalar", lambda wdt=wdt: nc.scalar.activation(sb_g[:, 0:wdt], ps1[:, 0:wdt], AF.Gelu, scale=1.0))
            sp.c("vector", lambda n0=n0, n1=n1, wdt=wdt: nc.vector.tensor_mul(mid_bf[:, n0:n1], ps0[:, 0:wdt], sb_g[:, 0:wdt]))
        nfull = DI // 128
        transpose_to(sp, nc, lambda j: midT[:, j * 128:(j + 1) * 128],
                     lambda j: mid_bf[:, j * 128:(j + 1) * 128], nfull, tbk)
        sp.c("tensor", lambda: nc.tensor.transpose(ps_tb[0:DI - nfull * 128, 0:128], mid_bf[:, nfull * 128:DI], identb[:, :]))
        sp.c("vector", lambda: nc.vector.tensor_copy(midT[0:DI - nfull * 128, nfull * 128:nfull * 128 + 128], ps_tb[0:DI - nfull * 128, 0:128]))
        for c in range(22):
            kr = 128 if c < 21 else DI - 21 * 128
            sp.d("sync", lambda c=c, kr=kr: nc.sync.dma_start(out=wtmp[0:kr, 0:1024], in_=W2d[c * 128:c * 128 + kr, :]), 1)
            sp.c("vector", lambda kr=kr: nc.vector.tensor_copy(w2s[0:kr, :], wtmp[0:kr, 0:1024]))

            def pe(c=c, kr=kr):
                out = []
                for nh in range(2):
                    pp = ps0 if nh == 0 else ps1
                    if c == 0:
                        out.append(nc.tensor.matmul(pp[:, :], ones_b[0:1, :], b2row[0:1, nh * 512:nh * 512 + 512], start=True, stop=False))
                    out.append(nc.tensor.matmul(pp[:, :], midT[0:kr, c * 128:c * 128 + 128],
                                                w2s[0:kr, nh * 512:nh * 512 + 512],
                                                start=False, stop=(c == 21)))
                return out
            sp.c("tensor", pe)
        for nh in range(2):
            sp.c("scalar", lambda nh=nh: nc.scalar.copy(yo[:, nh * 512:nh * 512 + 512], (ps0 if nh == 0 else ps1)[:, :]))
        sp.d("sync", lambda r0=r0: nc.sync.dma_start(out=xt[:, :], in_=y_d[r0:r0 + 128, :]), 1)
        sp.c("vector", lambda: nc.vector.tensor_add(yo[:, :], yo[:, :], xt[:, :]))
        sp.d("gpsimd", lambda tile=tile: nc.gpsimd.dma_start(out=xout[(tile - FT0) * 128:(tile - FT0) * 128 + 128, :], in_=yo[:, :]), 1)

    sp.build()
    st.close()
    return nc


_prog_cache = {}


def _get_prog(name, builder):
    if name not in _prog_cache:
        _prog_cache[name] = builder()
    return _prog_cache[name]


def _rope_l0():
    pos = np.arange(T, dtype=np.float64)
    inv = 1.0 / (10000.0 ** (np.arange(32, dtype=np.float64) / 32.0))
    ang = pos[:, None] * inv[None, :]
    C = np.concatenate([np.cos(ang), np.cos(ang)], 1).astype(np.float32)
    S = np.concatenate([np.sin(ang), np.sin(ang)], 1).astype(np.float32)
    return C, S


def _xpos_l1():
    DH = 64
    WIN = 256
    half = np.arange(0, DH, 2, dtype=np.float64)
    inv = 1.0 / (10000.0 ** (half / DH))
    tt = np.arange(T, dtype=np.float64)
    fr = tt[:, None] * inv[None, :]
    fr = np.concatenate([fr, fr], 1)
    base = (half + 0.4 * DH) / (1.4 * DH)
    power = (tt - T // 2) / (WIN // 2)
    sc = base[None, :] ** power[:, None]
    sc = np.concatenate([sc, sc], 1)
    s8 = np.sqrt(8.0)
    A = (np.cos(fr) * sc * s8).astype(np.float32)
    Bt = (np.sin(fr) * sc * s8).astype(np.float32)
    Ai = (np.cos(fr) / sc * s8).astype(np.float32)
    Bi = (np.sin(fr) / sc * s8).astype(np.float32)
    return A, Bt, Ai, Bi


def _run(nc, in_maps):
    res = run_bass_kernel_spmd(nc, in_maps, core_ids=list(range(NCORE)))
    return res.results


def kernel_unfused(**inp):
    f32 = np.float32
    cc = np.ascontiguousarray
    ones_r = np.ones((1, 128), f32)
    ones_rb = np.ones((1, 128), ml_dtypes.bfloat16)
    identb = np.eye(128, dtype=ml_dtypes.bfloat16)
    identf = np.eye(128, dtype=f32)

    x = np.asarray(inp["L0_attn_x"], f32)

    ncA = _get_prog("A", build_l0_attn)
    ropeC, ropeS = _rope_l0()
    maps = []
    for c in range(NCORE):
        b, qh = c // 2, c % 2
        maps.append(dict(
            xb=cc(x[b]), xq=cc(x[b, qh * 2048:(qh + 1) * 2048]),
            wq=inp["L0_attn_wq_w"], wk=inp["L0_attn_wk_w"], wv=inp["L0_attn_wv_w"], wo=inp["L0_attn_wo_w"],
            bqr=inp["L0_attn_wq_b"].reshape(1, D), bkr=inp["L0_attn_wk_b"].reshape(1, D),
            bvr=inp["L0_attn_wv_b"].reshape(1, D), bor=inp["L0_attn_wo_b"].reshape(1, D),
            ropeCq=cc(ropeC[qh * 2048:(qh + 1) * 2048] * 0.125), ropeSq=cc(ropeS[qh * 2048:(qh + 1) * 2048] * 0.125),
            ropeCk=ropeC, ropeSk=ropeS,
            onesd=ones_r, identd=identb))
    res = _run(ncA, maps)
    x1 = np.stack([np.concatenate([res[2 * b]["xo"], res[2 * b + 1]["xo"]], 0) for b in range(B)], 0)
    x1 = x1 + 0.0 if False else x1
    x1 = x1.reshape(B, T, D)

    ncB = _get_prog("B", build_l0_ffn)
    x1f = x1.reshape(B * T, D)
    maps = []
    for c in range(NCORE):
        maps.append(dict(
            x1=cc(x1f[c * 2048:(c + 1) * 2048]),
            ln_wd=inp["L0_ffn_ln_w"].reshape(1, D), ln_bd=inp["L0_ffn_ln_b"].reshape(1, D),
            router_w=inp["L0_ffn_router_w"], gate_w=inp["L0_ffn_gate_w"], up_w=inp["L0_ffn_up_w"],
            down_w=inp["L0_ffn_down_w"], sh_gate=inp["L0_ffn_sh_gate"], sh_up=inp["L0_ffn_sh_up"],
            sh_down=inp["L0_ffn_sh_down"],
            onesd=ones_r, identd=identb, identfd=identf))
    res = _run(ncB, maps)
    x2 = np.concatenate([res[c]["xo"] for c in range(NCORE)], 0).reshape(B, T, D)

    ncC = _get_prog("C", build_l1_attn)
    iota = np.arange(T, dtype=f32).reshape(1, T)
    maps = []
    for c in range(NCORE):
        b, qh = c // 2, c % 2
        maps.append(dict(
            xb=cc(x2[b]), xq=cc(x2[b, qh * 2048:(qh + 1) * 2048]),
            Wqd=inp["L1_attn_Wq"], Wkd=inp["L1_attn_Wk"],
            WvdT=cc(np.asarray(inp["L1_attn_Wvdown"]).T), Wvu=inp["L1_attn_Wvup"],
            iotad=iota, qidxd=cc((qh * 2048 + np.arange(2048, dtype=f32)).reshape(16, 128).T),
            onesd=ones_r, identd=identb))
    res = _run(ncC, maps)
    x3 = np.stack([np.concatenate([res[2 * b]["xo"], res[2 * b + 1]["xo"]], 0) for b in range(B)], 0)

    ncP = _get_prog("P", build_lnpre)
    x3f = x3.reshape(B * T, D)
    maps = [dict(xin=cc(x3f[c * 2048:(c + 1) * 2048])) for c in range(NCORE)]
    res = _run(ncP, maps)
    h = np.concatenate([res[c]["hout"] for c in range(NCORE)], 0).reshape(B, T, D)

    ncE = _get_prog("E", build_l1_inner)
    A, Bt, Ai, Bi = _xpos_l1()
    iota5 = np.arange(512, dtype=f32).reshape(1, 512)
    ilc = np.stack([np.arange(128, dtype=f32), 128 + np.arange(128, dtype=f32)], 1)
    ilc2 = ilc + 256.0
    for layer in range(6):
        maps = []
        for c in range(NCORE):
            b, ch = c // 2, c % 2
            s = ch * 2048
            if ch == 0:
                xin = np.concatenate([np.zeros((768, D), f32), h[b, 0:2048]], 0)
            else:
                xin = cc(h[b, 2048 - 768:4096])
            gidx = np.clip(np.arange(s - 768, s + 2048), 0, T - 1)
            thr = np.zeros((128, 17), f32)
            if ch == 0:
                thr[:, 1] = 256.0
                thr[:, 2] = 256.0
            maps.append(dict(
                xin=xin,
                Wqkv=cc(inp["L1_ffn_Wqkv"][layer]), Wod=cc(inp["L1_ffn_Wo"][layer]),
                W1d=cc(inp["L1_ffn_W1"][layer]), b1d=cc(inp["L1_ffn_b1"][layer].reshape(1, -1)),
                W2d=cc(inp["L1_ffn_W2"][layer]), b2d=cc(inp["L1_ffn_b2"][layer].reshape(1, -1)),
                lngd=cc(inp["L1_ffn_ln_g"][layer].reshape(1, -1)), lnbd=cc(inp["L1_ffn_ln_b"][layer].reshape(1, -1)),
                qscd=cc(inp["L1_ffn_q_scale"][layer].reshape(1, -1)), kscd=cc(inp["L1_ffn_k_scale"][layer].reshape(1, -1)),
                tabA=cc(A[gidx]), tabB=cc(Bt[gidx]), tabAi=cc(Ai[gidx]), tabBi=cc(Bi[gidx]),
                iota5d=iota5, ilcd=cc(ilc), ilc2d=cc(ilc2), thrd=thr,
                onesd=ones_r, onesbd=ones_rb, identd=identb))
        res = _run(ncE, maps)
        h = np.stack([np.concatenate([res[2 * b]["xout"], res[2 * b + 1]["xout"]], 0) for b in range(B)], 0)

    ncQ = _get_prog("Q", build_lnpost)
    hf = h.reshape(B * T, D)
    maps = [dict(x3in=cc(x3f[c * 2048:(c + 1) * 2048]), hin=cc(hf[c * 2048:(c + 1) * 2048])) for c in range(NCORE)]
    res = _run(ncQ, maps)
    x4 = np.concatenate([res[c]["xout"] for c in range(NCORE)], 0).reshape(B, T, D)
    return x4


def build_fused(debug=False):
    nc = bass.Bass("TRN2", target_bir_lowering=False)
    NT = T
    DI_ = DI

    xb = nc.dram_tensor("xb", [T, D], F32, kind="ExternalInput").ap()
    wq = nc.dram_tensor("wq", [D, D], F32, kind="ExternalInput").ap()
    wk = nc.dram_tensor("wk", [D, D], F32, kind="ExternalInput").ap()
    wv = nc.dram_tensor("wv", [D, D], F32, kind="ExternalInput").ap()
    wo = nc.dram_tensor("wo", [D, D], F32, kind="ExternalInput").ap()
    bqr = nc.dram_tensor("bqr", [1, D], F32, kind="ExternalInput").ap()
    bkr = nc.dram_tensor("bkr", [1, D], F32, kind="ExternalInput").ap()
    bvr = nc.dram_tensor("bvr", [1, D], F32, kind="ExternalInput").ap()
    bor = nc.dram_tensor("bor", [1, D], F32, kind="ExternalInput").ap()
    ropeCq = nc.dram_tensor("ropeCq", [T, 64], F32, kind="ExternalInput").ap()
    ropeSq = nc.dram_tensor("ropeSq", [T, 64], F32, kind="ExternalInput").ap()
    ropeCk = nc.dram_tensor("ropeCk", [T, 64], F32, kind="ExternalInput").ap()
    ropeSk = nc.dram_tensor("ropeSk", [T, 64], F32, kind="ExternalInput").ap()
    ln_wd = nc.dram_tensor("ln_wd", [1, D], F32, kind="ExternalInput").ap()
    ln_bd = nc.dram_tensor("ln_bd", [1, D], F32, kind="ExternalInput").ap()
    router_w = nc.dram_tensor("router_w", [D, 8], F32, kind="ExternalInput").ap()
    gate_w = nc.dram_tensor("gate_w", [8, D, 2048], F32, kind="ExternalInput").ap()
    up_w = nc.dram_tensor("up_w", [8, D, 2048], F32, kind="ExternalInput").ap()
    down_w = nc.dram_tensor("down_w", [8, 2048, D], F32, kind="ExternalInput").ap()
    sh_gate = nc.dram_tensor("sh_gate", [D, 2048], F32, kind="ExternalInput").ap()
    sh_up = nc.dram_tensor("sh_up", [D, 2048], F32, kind="ExternalInput").ap()
    sh_down = nc.dram_tensor("sh_down", [2048, D], F32, kind="ExternalInput").ap()
    Wqd = nc.dram_tensor("Wqd", [D, 256], F32, kind="ExternalInput").ap()
    Wkd = nc.dram_tensor("Wkd", [D, 256], F32, kind="ExternalInput").ap()
    WvdT = nc.dram_tensor("WvdT", [256, D], F32, kind="ExternalInput").ap()
    Wvu = nc.dram_tensor("Wvu", [256, D], F32, kind="ExternalInput").ap()
    iotad = nc.dram_tensor("iotad", [1, T], F32, kind="ExternalInput").ap()
    qidxd = nc.dram_tensor("qidxd", [128, 32], F32, kind="ExternalInput").ap()
    WqkvA = nc.dram_tensor("WqkvA", [6, D, 3072], F32, kind="ExternalInput").ap()
    WoA = nc.dram_tensor("WoA", [6, D, D], F32, kind="ExternalInput").ap()
    W1A = nc.dram_tensor("W1A", [6, D, 2 * DI_], F32, kind="ExternalInput").ap()
    b1A = nc.dram_tensor("b1A", [6, 2 * DI_], F32, kind="ExternalInput").ap()
    W2A = nc.dram_tensor("W2A", [6, DI_, D], F32, kind="ExternalInput").ap()
    b2A = nc.dram_tensor("b2A", [6, D], F32, kind="ExternalInput").ap()
    lngA = nc.dram_tensor("lngA", [6, D], F32, kind="ExternalInput").ap()
    lnbA = nc.dram_tensor("lnbA", [6, D], F32, kind="ExternalInput").ap()
    qscA = nc.dram_tensor("qscA", [6, 64], F32, kind="ExternalInput").ap()
    kscA = nc.dram_tensor("kscA", [6, 64], F32, kind="ExternalInput").ap()
    tabA_d = nc.dram_tensor("tabA_d", [T, 64], F32, kind="ExternalInput").ap()
    tabB_d = nc.dram_tensor("tabB_d", [T, 64], F32, kind="ExternalInput").ap()
    tabAi_d = nc.dram_tensor("tabAi_d", [T, 64], F32, kind="ExternalInput").ap()
    tabBi_d = nc.dram_tensor("tabBi_d", [T, 64], F32, kind="ExternalInput").ap()
    iota5d = nc.dram_tensor("iota5d", [1, 512], F32, kind="ExternalInput").ap()
    colAd = nc.dram_tensor("colAd", [128, 32], F32, kind="ExternalInput").ap()
    colBd = nc.dram_tensor("colBd", [128, 32], F32, kind="ExternalInput").ap()
    onesd = nc.dram_tensor("onesd", [1, 128], F32, kind="ExternalInput").ap()
    onesbd = nc.dram_tensor("onesbd", [1, 128], BF16, kind="ExternalInput").ap()
    identd = nc.dram_tensor("identd", [128, 128], BF16, kind="ExternalInput").ap()
    identfd = nc.dram_tensor("identfd", [128, 128], F32, kind="ExternalInput").ap()
    xod = nc.dram_tensor("xod", [T, D], F32, kind="ExternalOutput").ap()

    ikind = "ExternalOutput" if debug else "Internal"
    x1_d = nc.dram_tensor("x1_d", [T, D], F32, kind=ikind).ap()
    x2_d = nc.dram_tensor("x2_d", [T, D], F32, kind=ikind).ap()
    x3_d = nc.dram_tensor("x3_d", [T, D], F32, kind=ikind).ap()
    h0_d = nc.dram_tensor("h0_d", [T, D], F32, kind=ikind).ap()
    h1_d = nc.dram_tensor("h1_d", [T, D], F32, kind=ikind).ap()
    y_d = nc.dram_tensor("y_d", [T, D], F32, kind="Internal").ap()
    QT_d = nc.dram_tensor("QT_d", [8, 128, T], BF16, kind="Internal").ap()
    KT_d = nc.dram_tensor("KT_d", [8, 128, T], BF16, kind="Internal").ap()
    V_d = nc.dram_tensor("V_d", [T, D], BF16, kind="Internal").ap()
    YT_d = nc.dram_tensor("YT_d", [8, 128, T], BF16, kind="Internal").ap()
    hmoe_d = nc.dram_tensor("hmoe_d", [T, D], F32, kind="Internal").ap()
    XFT_d = nc.dram_tensor("XFT_d", [8, 128, T], BF16, kind="Internal").ap()
    wT_d = nc.dram_tensor("wT_d", [8, T], F32, kind="Internal").ap()
    acc_d = nc.dram_tensor("acc_d", [T, D], BF16, kind="Internal").ap()
    QT2_d = nc.dram_tensor("QT2_d", [2, 128, T], BF16, kind="Internal").ap()
    KT2_d = nc.dram_tensor("KT2_d", [2, 128, T], BF16, kind="Internal").ap()

    sp = SP(nc)
    st_glob = contextlib.ExitStack()
    sbg = lambda n, s, dt=F32: st_glob.enter_context(nc.sbuf_tensor(n, s, dt))
    pst = lambda n, s, dt=F32: st_glob.enter_context(nc.psum_tensor(n, s, dt))

    ones_f = sbg("ones_f", [1, 128])
    ones_b = sbg("ones_b", [1, 128], BF16)
    identb = sbg("identb", [128, 128], BF16)
    identf = sbg("identf", [128, 128], F32)
    epsc = sbg("epsc", [128, 1])
    t = dict(mean=sbg("mean", [128, 1]), var=sbg("var", [128, 1]), rstd=sbg("rstd", [128, 1]),
             s2=sbg("s2", [128, 1]), xc=sbg("xc", [128, D]), sq=sbg("sq", [128, D]), epsc=epsc)
    xt = sbg("xt", [128, D])
    xn = sbg("xn", [128, D])
    xnb = sbg("xnb", [128, D], BF16)
    xnT = sbg("xnT", [128, 8 * 128], BF16)
    yo = sbg("yo", [128, D])
    wtmp = sbg("wtmp", [128, 2048])

    ps0 = pst("ps0", [128, 512])
    ps1 = pst("ps1", [128, 512])
    ps2 = pst("ps2", [128, 512])
    ps3 = pst("ps3", [128, 512])
    ps_y = pst("ps_y", [128, 512])
    ps_m = pst("ps_m", [128, 512])
    ps_tb = pst("ps_tb", [128, 512], BF16)
    tbk = {'identb': identb, 'ps': ps_tb}
    pslist = [ps0, ps1, ps2, ps3]

    sp.d("sync", lambda: [nc.sync.dma_start(out=ones_f[:, :], in_=onesd[:, :]),
                          nc.sync.dma_start(out=ones_b[:, :], in_=onesbd[:, :]),
                          nc.sync.dma_start(out=identb[:, :], in_=identd[:, :]),
                          nc.sync.dma_start(out=identf[:, :], in_=identfd[:, :])], 4)
    sp.c("vector", lambda: nc.vector.memset(epsc[:, :], EPS))

    def ln_tile_from(x_src, tile):
        r0 = tile * 128
        sp.d("sync", lambda r0=r0: nc.sync.dma_start(out=xt[:, :], in_=x_src[r0:r0 + 128, :]), 1)
        ln_plain(sp, nc, xn[:, :], xt[:, :], D, t)
        sp.c("vector", lambda: nc.vector.tensor_copy(xnb[:, :], xn[:, :]))
        transpose_to(sp, nc, lambda j: xnT[:, j * 128:(j + 1) * 128],
                     lambda j: xnb[:, j * 128:(j + 1) * 128], 8, tbk)

    def _stage_a():
        with contextlib.ExitStack() as sl:
            sb = lambda n, s, dt=F32: sl.enter_context(nc.sbuf_tensor(n, s, dt))
            warena = sb("A_warena", [128, 16384], BF16)
            brow = sb("A_brow", [1, D])
            biasb = sb("A_biasb", [128, 2 * D])
            proj = sb("A_proj", [128, D])
            rot = sb("A_rot", [128, D])
            rtmp = sb("A_rtmp", [128, D])
            ctab = sb("A_ctab", [128, 64])
            stab = sb("A_stab", [128, 64])
            projb = sb("A_projb", [128, D], BF16)
            stage = sb("A_stage", [128, 8 * 128], BF16)
            QTp = sb("A_QTp", [128, T], BF16)
            KTp = sb("A_KTp", [128, T], BF16)
            Vp = sb("A_Vp", [128, 32 * 128], BF16)
            S_f = sb("A_S_f", [128, T])
            A_bf = sb("A_A_bf", [128, T], BF16)
            ATb = sb("A_ATb", [128, 16384], BF16)
            YTs = sb("A_YTs", [128, T], BF16)
            ytT = sb("A_ytT", [128, 8 * 128], BF16)

            def bias_to(dst_off, b_dram):
                sp.d("sync", lambda b_dram=b_dram: nc.sync.dma_start(out=brow[:, :], in_=b_dram[:, :]), 1)
                bcast_row(sp, nc, biasb[:, dst_off:dst_off + D], brow, D, ones_f, ps_m)

            def project(wcol, bias_off, out_tile):
                for nh in range(2):
                    def pe(nh=nh, wcol=wcol):
                        return [nc.tensor.matmul((ps0 if nh == 0 else ps1)[:, :],
                                                 xnT[:, c * 128:(c + 1) * 128],
                                                 warena[:, wcol + c * 1024 + nh * 512: wcol + c * 1024 + nh * 512 + 512],
                                                 start=(c == 0), stop=(c == 7)) for c in range(8)]
                    sp.c("tensor", pe)
                for nh in range(2):
                    sp.c("vector", lambda nh=nh, bias_off=bias_off, out_tile=out_tile: nc.vector.tensor_add(
                        out_tile[:, nh * 512:nh * 512 + 512], (ps0 if nh == 0 else ps1)[:, :],
                        biasb[:, bias_off + nh * 512: bias_off + nh * 512 + 512]))

            load_w_bf16(sp, nc, warena, wk, D, D, wtmp, 0)
            load_w_bf16(sp, nc, warena, wv, D, D, wtmp, 8192)
            bias_to(0, bkr)
            bias_to(D, bvr)
            for tile in range(32):
                ln_tile_from(xb, tile)
                project(0, 0, proj)
                sp.d("sync", lambda tile=tile: [nc.sync.dma_start(out=ctab[:, :], in_=ropeCk[tile * 128:tile * 128 + 128, :]),
                                                nc.sync.dma_start(out=stab[:, :], in_=ropeSk[tile * 128:tile * 128 + 128, :])], 2)
                rope_apply(sp, nc, proj[:, :], rot, ctab, stab, 16, 64, rtmp)
                sp.c("vector", lambda: nc.vector.tensor_copy(projb[:, :], proj[:, :]))
                transpose_to(sp, nc, lambda j: stage[:, j * 128:(j + 1) * 128],
                             lambda j: projb[:, j * 128:(j + 1) * 128], 8, tbk)

                def store_kt(tile=tile):
                    src = stage[:, :].rearrange("p (r c) -> p r c", c=128)
                    dst = KT_d[:, :, tile * 128:tile * 128 + 128].rearrange("r p c -> p r c")
                    return nc.gpsimd.dma_start(out=dst, in_=src)
                sp.d("gpsimd", store_kt, 1)
                project(8192, D, proj)
                sp.c("vector", lambda: nc.vector.tensor_copy(projb[:, :], proj[:, :]))
                sp.d("gpsimd", lambda tile=tile: nc.gpsimd.dma_start(out=V_d[tile * 128:tile * 128 + 128, :], in_=projb[:, :]), 1)

            load_w_bf16(sp, nc, warena, wq, D, D, wtmp, 0)
            bias_to(0, bqr)
            for tile in range(32):
                ln_tile_from(xb, tile)
                project(0, 0, proj)
                sp.d("sync", lambda tile=tile: [nc.sync.dma_start(out=ctab[:, :], in_=ropeCq[tile * 128:tile * 128 + 128, :]),
                                                nc.sync.dma_start(out=stab[:, :], in_=ropeSq[tile * 128:tile * 128 + 128, :])], 2)
                rope_apply(sp, nc, proj[:, :], rot, ctab, stab, 16, 64, rtmp)
                sp.c("vector", lambda: nc.vector.tensor_copy(projb[:, :], proj[:, :]))
                transpose_to(sp, nc, lambda j: stage[:, j * 128:(j + 1) * 128],
                             lambda j: projb[:, j * 128:(j + 1) * 128], 8, tbk)

                def store_qt(tile=tile):
                    src = stage[:, :].rearrange("p (r c) -> p r c", c=128)
                    dst = QT_d[:, :, tile * 128:tile * 128 + 128].rearrange("r p c -> p r c")
                    return nc.gpsimd.dma_start(out=dst, in_=src)
                sp.d("gpsimd", store_qt, 1)

            for pair in range(8):
                sp.d("sync", lambda pair=pair: [nc.sync.dma_start(out=QTp[:, :], in_=QT_d[pair, :, :]),
                                                nc.sync.dma_start(out=KTp[:, :], in_=KT_d[pair, :, :])], 2)

                def load_vp(pair=pair):
                    src = V_d[:, pair * 128:(pair + 1) * 128].rearrange("(tt q) c -> q tt c", q=128)
                    dst = Vp[:, :].rearrange("q (tt c) -> q tt c", c=128)
                    return nc.sync.dma_start(out=dst, in_=src)
                sp.d("sync", load_vp, 1)
                for hh in range(2):
                    hb = hh * 64
                    for g in range(8):
                        for qt in range(4):
                            qc = g * 512 + qt * 128
                            for kh in range(2):
                                def pe(kh=kh, qc=qc, hb=hb):
                                    return [nc.tensor.matmul(pslist[j][:, :],
                                                             QTp[hb:hb + 64, qc:qc + 128],
                                                             KTp[hb:hb + 64, (kh * 4 + j) * 512:(kh * 4 + j) * 512 + 512],
                                                             start=True, stop=True) for j in range(4)]
                                sp.c("tensor", pe)

                                def ev(kh=kh):
                                    return [nc.scalar.copy(S_f[:, (kh * 4 + j) * 512:(kh * 4 + j) * 512 + 512], pslist[j][:, :])
                                            for j in range(4)]
                                sp.c("scalar", ev)
                            softmax_rows(sp, nc, A_bf[:, :], S_f[:, :], t)
                            transpose_to(sp, nc,
                                         lambda j, qt=qt: ATb[:, j * 512 + qt * 128: j * 512 + qt * 128 + 128],
                                         lambda j: A_bf[:, j * 128:(j + 1) * 128], 32, tbk)

                        def av(hb=hb):
                            return [nc.tensor.matmul(ps_y[0:64, :], Vp[:, kt * 128 + hb: kt * 128 + hb + 64],
                                                     ATb[:, kt * 512:(kt + 1) * 512],
                                                     start=(kt == 0), stop=(kt == 31)) for kt in range(32)]
                        sp.c("tensor", av)
                        sp.c("scalar", lambda hb=hb, g=g: nc.scalar.copy(YTs[hb:hb + 64, g * 512:(g + 1) * 512], ps_y[0:64, :]))
                sp.d("gpsimd", lambda pair=pair: nc.gpsimd.dma_start(out=YT_d[pair, :, :], in_=YTs[:, :]), 1)

            load_w_bf16(sp, nc, warena, wo, D, D, wtmp, 0)
            bias_to(0, bor)
            for tile in range(32):
                def load_yt(tile=tile):
                    src = YT_d[:, :, tile * 128:tile * 128 + 128].rearrange("r p c -> p r c")
                    dst = ytT[:, :].rearrange("p (r c) -> p r c", c=128)
                    return nc.sync.dma_start(out=dst, in_=src)
                sp.d("sync", load_yt, 1)
                for nh in range(2):
                    def pe(nh=nh):
                        return [nc.tensor.matmul((ps0 if nh == 0 else ps1)[:, :],
                                                 ytT[:, c * 128:(c + 1) * 128],
                                                 warena[:, c * 1024 + nh * 512: c * 1024 + nh * 512 + 512],
                                                 start=(c == 0), stop=(c == 7)) for c in range(8)]
                    sp.c("tensor", pe)
                for nh in range(2):
                    sp.c("vector", lambda nh=nh: nc.vector.tensor_add(yo[:, nh * 512:nh * 512 + 512],
                                                                      (ps0 if nh == 0 else ps1)[:, :],
                                                                      biasb[:, nh * 512:nh * 512 + 512]))
                ln_plain(sp, nc, yo[:, :], yo[:, :], D, t)
                sp.d("sync", lambda tile=tile: nc.sync.dma_start(out=xt[:, :], in_=xb[tile * 128:tile * 128 + 128, :]), 1)
                sp.c("vector", lambda: nc.vector.tensor_add(yo[:, :], yo[:, :], xt[:, :]))
                sp.d("gpsimd", lambda tile=tile: nc.gpsimd.dma_start(out=x1_d[tile * 128:tile * 128 + 128, :], in_=yo[:, :]), 1)

    _stage_a()

    def _stage_b():
        with contextlib.ExitStack() as sl:
            sb = lambda n, s, dt=F32: sl.enter_context(nc.sbuf_tensor(n, s, dt))
            g_b = sb("B_g_b", [128, 8 * 2048], BF16)
            u_b = sb("B_u_b", [128, 8 * 2048], BF16)
            d_b = sb("B_d_b", [128, 16 * 1024], BF16)
            accg = sb("B_accg", [128, 4 * 1024], BF16)
            XFTg = sb("B_XFTg", [128, 8 * 512], BF16)
            hm = sb("B_hm", [128, 16 * 512], BF16)
            wB = sb("B_wB", [128, 512])
            wrow = sb("B_wrow", [1, 512])
            sb_a = sb("B_sb_a", [128, 512])
            sb_b = sb("B_sb_b", [128, 512])
            sb_c = sb("B_sb_c", [128, 512], BF16)
            lrow = sb("B_lrow", [1, D])
            hT = sb("B_hT", [128, D])
            xf = sb("B_xf", [128, D])
            xfb = sb("B_xfb", [128, D], BF16)
            lnwB = sb("B_lnwB", [128, D])
            lnbB = sb("B_lnbB", [128, D])
            rw = sb("B_rw", [128, 64], BF16)
            r8 = sb("B_r8", [8, 128])
            lg = sb("B_lg", [128, 8])
            wcols = sb("B_wcols", [128, 8])
            sig = sb("B_sig", [128, 1])

            sp.d("sync", lambda: nc.sync.dma_start(out=lrow[:, :], in_=ln_wd[:, :]), 1)
            bcast_row(sp, nc, lnwB[:, :], lrow, D, ones_f, ps_m)
            sp.d("sync", lambda: nc.sync.dma_start(out=lrow[:, :], in_=ln_bd[:, :]), 1)
            bcast_row(sp, nc, lnbB[:, :], lrow, D, ones_f, ps_m)
            sp.d("sync", lambda: nc.sync.dma_start(out=wtmp[:, 0:64].rearrange("p (c e) -> p c e", e=8),
                                                   in_=router_w.rearrange("(c p) e -> p c e", p=128)), 1)
            sp.c("vector", lambda: nc.vector.tensor_copy(rw[:, :], wtmp[:, 0:64]))

            for tile in range(32):
                r0 = tile * 128
                sp.d("sync", lambda r0=r0: nc.sync.dma_start(out=xt[:, :], in_=x1_d[r0:r0 + 128, :]), 1)
                ln_plain(sp, nc, hT[:, :], xt[:, :], D, t)
                sp.d("gpsimd", lambda r0=r0: nc.gpsimd.dma_start(out=hmoe_d[r0:r0 + 128, :], in_=hT[:, :]), 1)
                ln_plain(sp, nc, xf[:, :], hT[:, :], D, t)
                sp.c("vector", lambda: nc.vector.tensor_mul(xf[:, :], xf[:, :], lnwB[:, :]))
                sp.c("vector", lambda: nc.vector.tensor_add(xf[:, :], xf[:, :], lnbB[:, :]))
                sp.c("vector", lambda: nc.vector.tensor_copy(xfb[:, :], xf[:, :]))
                transpose_to(sp, nc, lambda j: xnT[:, j * 128:(j + 1) * 128],
                             lambda j: xfb[:, j * 128:(j + 1) * 128], 8, tbk)

                def store_xft(r0=r0):
                    src = xnT[:, :].rearrange("p (r c) -> p r c", c=128)
                    dst = XFT_d[:, :, r0:r0 + 128].rearrange("r p c -> p r c")
                    return nc.gpsimd.dma_start(out=dst, in_=src)
                sp.d("gpsimd", store_xft, 1)

                def r_pe():
                    return [nc.tensor.matmul(ps_m[:, 0:8], xnT[:, c * 128:(c + 1) * 128], rw[:, c * 8:(c + 1) * 8],
                                             start=(c == 0), stop=(c == 7)) for c in range(8)]
                sp.c("tensor", r_pe)
                sp.c("scalar", lambda: nc.scalar.copy(lg[:, :], ps_m[:, 0:8]))
                sp.c("vector", lambda: nc.vector.reduce_max(t['mean'][:, 0:1], lg[:, :], axis=AX.X))
                sp.c("scalar", lambda: nc.scalar.activation(sig[:, 0:1], t['mean'][:, 0:1], AF.Sigmoid, scale=1.0))
                sp.c("vector", lambda: nc.vector.tensor_scalar(wcols[:, :], lg[:, :], t['mean'][:, 0:1], None, ALU.is_equal))
                sp.c("vector", lambda: nc.vector.tensor_scalar(wcols[:, :], wcols[:, :], sig[:, 0:1], None, ALU.mult))
                sp.c("tensor", lambda: nc.tensor.matmul(ps_m[0:8, 0:128], wcols[:, 0:8], identf[:, :], start=True, stop=True))
                sp.c("scalar", lambda: nc.scalar.copy(r8[:, :], ps_m[0:8, 0:128]))
                sp.d("gpsimd", lambda r0=r0: nc.gpsimd.dma_start(out=wT_d[:, r0:r0 + 128], in_=r8[:, :]), 1)

            sp.c("vector", lambda: nc.vector.memset(accg[:, :], 0.0))
            for g in range(8):
                def zst(g=g):
                    src = accg[:, :].rearrange("p (s c) -> p s c", c=1024)
                    dst = acc_d[g * 512:(g + 1) * 512, :].rearrange("(s p) c -> p s c", p=128)
                    return nc.gpsimd.dma_start(out=dst, in_=src)
                sp.d("gpsimd", zst, 1)

            def expert(e_idx, g_dram, u_dram, d_dram, use_w):
                for c in range(8):
                    sp.d("sync", lambda c=c, g_dram=g_dram: nc.sync.dma_start(out=wtmp[:, :], in_=g_dram[c * 128:(c + 1) * 128, :]), 1)
                    sp.c("vector", lambda c=c: nc.vector.tensor_copy(g_b[:, c * 2048:(c + 1) * 2048], wtmp[:, :]))
                for c in range(8):
                    sp.d("sync", lambda c=c, u_dram=u_dram: nc.sync.dma_start(out=wtmp[:, :], in_=u_dram[c * 128:(c + 1) * 128, :]), 1)
                    sp.c("vector", lambda c=c: nc.vector.tensor_copy(u_b[:, c * 2048:(c + 1) * 2048], wtmp[:, :]))
                for c in range(16):
                    sp.d("sync", lambda c=c, d_dram=d_dram: nc.sync.dma_start(out=wtmp[:, 0:1024], in_=d_dram[c * 128:(c + 1) * 128, :]), 1)
                    sp.c("vector", lambda c=c: nc.vector.tensor_copy(d_b[:, c * 1024:(c + 1) * 1024], wtmp[:, 0:1024]))
                for g in range(8):
                    g0 = g * 512

                    def load_xft(g0=g0):
                        src = XFT_d[:, :, g0:g0 + 512].rearrange("r p c -> p r c")
                        dst = XFTg[:, :].rearrange("p (r c) -> p r c", c=512)
                        return nc.sync.dma_start(out=dst, in_=src)
                    sp.d("sync", load_xft, 1)
                    if use_w:
                        sp.d("sync", lambda g0=g0, e_idx=e_idx: nc.sync.dma_start(out=wrow[:, :], in_=wT_d[e_idx:e_idx + 1, g0:g0 + 512]), 1)
                        sp.c("tensor", lambda: nc.tensor.matmul(ps_m[:, :], ones_f[0:1, :], wrow[0:1, :], start=True, stop=True))
                        sp.c("scalar", lambda: nc.scalar.copy(wB[:, :], ps_m[:, :]))
                    for i in range(16):
                        def zg_pe(i=i):
                            return [nc.tensor.matmul(ps0[:, :],
                                                     g_b[:, kc * 2048 + i * 128: kc * 2048 + i * 128 + 128],
                                                     XFTg[:, kc * 512:(kc + 1) * 512],
                                                     start=(kc == 0), stop=(kc == 7)) for kc in range(8)]
                        sp.c("tensor", zg_pe)

                        def zu_pe(i=i):
                            return [nc.tensor.matmul(ps1[:, :],
                                                     u_b[:, kc * 2048 + i * 128: kc * 2048 + i * 128 + 128],
                                                     XFTg[:, kc * 512:(kc + 1) * 512],
                                                     start=(kc == 0), stop=(kc == 7)) for kc in range(8)]
                        sp.c("tensor", zu_pe)
                        if use_w:
                            sp.c("vector", lambda: nc.vector.tensor_mul(sb_a[:, :], ps0[:, :], wB[:, :]))
                            sp.c("scalar", lambda: nc.scalar.activation(sb_b[:, :], sb_a[:, :], AF.Silu, scale=1.0))
                            sp.c("vector", lambda: nc.vector.tensor_mul(sb_a[:, :], ps1[:, :], wB[:, :]))
                            sp.c("vector", lambda i=i: nc.vector.tensor_mul(hm[:, i * 512:(i + 1) * 512], sb_b[:, :], sb_a[:, :]))
                        else:
                            sp.c("scalar", lambda: nc.scalar.activation(sb_b[:, :], ps0[:, :], AF.Silu, scale=1.0))
                            sp.c("vector", lambda i=i: nc.vector.tensor_mul(hm[:, i * 512:(i + 1) * 512], sb_b[:, :], ps1[:, :]))
                    def acc_ld(g0=g0):
                        src = acc_d[g0:g0 + 512, :].rearrange("(s p) c -> p s c", p=128)
                        dst = accg[:, :].rearrange("p (s c) -> p s c", c=1024)
                        return nc.sync.dma_start(out=dst, in_=src)
                    sp.d("sync", acc_ld, 1)
                    for s4 in range(4):
                        for nh in range(2):
                            def dn_pe(s4=s4, nh=nh):
                                return [nc.tensor.matmul((ps2 if nh == 0 else ps3)[:, :],
                                                         hm[:, i * 512 + s4 * 128: i * 512 + s4 * 128 + 128],
                                                         d_b[:, i * 1024 + nh * 512: i * 1024 + nh * 512 + 512],
                                                         start=(i == 0), stop=(i == 15)) for i in range(16)]
                            sp.c("tensor", dn_pe)
                        for nh in range(2):
                            sp.c("scalar", lambda nh=nh: nc.scalar.copy(sb_c[:, :], (ps2 if nh == 0 else ps3)[:, :]))

                            def accadd(s4=s4, nh=nh):
                                col = s4 * 1024 + nh * 512
                                return nc.vector.tensor_add(accg[:, col:col + 512], accg[:, col:col + 512], sb_c[:, :])
                            sp.c("vector", accadd)

                    def acc_st(g0=g0):
                        src = accg[:, :].rearrange("p (s c) -> p s c", c=1024)
                        dst = acc_d[g0:g0 + 512, :].rearrange("(s p) c -> p s c", p=128)
                        return nc.gpsimd.dma_start(out=dst, in_=src)
                    sp.d("gpsimd", acc_st, 1)

            for e in range(8):
                expert(e, gate_w[e, :, :], up_w[e, :, :], down_w[e, :, :], True)
            expert(0, sh_gate, sh_up, sh_down, False)

            for tile in range(32):
                r0 = tile * 128
                sp.d("sync", lambda r0=r0: nc.sync.dma_start(out=hT[:, :], in_=hmoe_d[r0:r0 + 128, :]), 1)
                sp.d("sync", lambda r0=r0: nc.sync.dma_start(out=xfb[:, :], in_=acc_d[r0:r0 + 128, :]), 1)
                sp.c("vector", lambda: nc.vector.tensor_copy(xf[:, :], xfb[:, :]))
                sp.c("vector", lambda: nc.vector.tensor_add(xf[:, :], xf[:, :], hT[:, :]))
                ln_plain(sp, nc, xf[:, :], xf[:, :], D, t)
                sp.d("sync", lambda r0=r0: nc.sync.dma_start(out=xt[:, :], in_=x1_d[r0:r0 + 128, :]), 1)
                sp.c("vector", lambda: nc.vector.tensor_add(xf[:, :], xf[:, :], xt[:, :]))
                sp.d("gpsimd", lambda r0=r0: nc.gpsimd.dma_start(out=x2_d[r0:r0 + 128, :], in_=xf[:, :]), 1)

    _stage_b()

    def _stage_c():
        with contextlib.ExitStack() as sl:
            sb = lambda n, s, dt=F32: sl.enter_context(nc.sbuf_tensor(n, s, dt))
            wq_b = sb("C_wq_b", [128, 8 * 256], BF16)
            wk_b = sb("C_wk_b", [128, 8 * 256], BF16)
            wvdT_b = sb("C_wvdT_b", [128, 2 * 1024], BF16)
            wvu_b = sb("C_wvu_b", [128, 2 * 1024], BF16)
            M_b = sb("C_M_b", [128, 8 * 1024], BF16)
            iob = sb("C_iob", [128, T])
            qix = sb("C_qix", [128, 32])
            msk5 = sb("C_msk5", [128, 512])
            tmp5 = sb("C_tmp5", [128, 512])
            stage = sb("C_stage", [128, 1280], BF16)
            KT_sb = sb("C_KT_sb", [128, 2 * T], BF16)
            V_sb = sb("C_V_sb", [128, 32 * 1024], BF16)
            qtT = sb("C_qtT", [128, 2 * 128], BF16)
            S_f = sb("C_S_f", [128, T])
            A_bf = sb("C_A_bf", [128, T], BF16)
            ATb = sb("C_ATb", [128, 32 * 128], BF16)

            for kg in range(8):
                sp.d("sync", lambda kg=kg: nc.sync.dma_start(out=wtmp[0:1, 0:512], in_=iotad[0:1, kg * 512:(kg + 1) * 512]), 1)
                bcast_row(sp, nc, iob[:, kg * 512:(kg + 1) * 512], wtmp, 512, ones_f, ps_m)
            sp.d("sync", lambda: nc.sync.dma_start(out=qix[:, :], in_=qidxd[:, :]), 1)
            load_w_bf16(sp, nc, wq_b, Wqd, D, 256, wtmp, 0)
            load_w_bf16(sp, nc, wk_b, Wkd, D, 256, wtmp, 0)
            load_w_bf16(sp, nc, wvdT_b, WvdT, 256, D, wtmp, 0)
            load_w_bf16(sp, nc, wvu_b, Wvu, 256, D, wtmp, 0)
            for it in range(8):
                for nh in range(2):
                    def pe(it=it, nh=nh):
                        return [nc.tensor.matmul(ps0[:, :],
                                                 wvdT_b[:, c * 1024 + it * 128: c * 1024 + it * 128 + 128],
                                                 wvu_b[:, c * 1024 + nh * 512: c * 1024 + nh * 512 + 512],
                                                 start=(c == 0), stop=(c == 1)) for c in range(2)]
                    sp.c("tensor", pe)
                    sp.c("scalar", lambda it=it, nh=nh: nc.scalar.copy(M_b[:, it * 1024 + nh * 512: it * 1024 + nh * 512 + 512], ps0[:, :]))

            for tile in range(32):
                ln_tile_from(x2_d, tile)
                for ck in range(2):
                    def pe(ck=ck):
                        return [nc.tensor.matmul(ps0[:, 0:128],
                                                 wk_b[:, c * 256 + ck * 128: c * 256 + ck * 128 + 128],
                                                 xnT[:, c * 128:(c + 1) * 128],
                                                 start=(c == 0), stop=(c == 7)) for c in range(8)]
                    sp.c("tensor", pe)
                    sp.c("scalar", lambda ck=ck: nc.scalar.copy(stage[:, ck * 128:(ck + 1) * 128], ps0[:, 0:128]))

                def store_kt(tile=tile):
                    src = stage[:, 0:256].rearrange("p (r c) -> p r c", c=128)
                    dst = KT2_d[:, :, tile * 128:tile * 128 + 128].rearrange("r p c -> p r c")
                    return nc.gpsimd.dma_start(out=dst, in_=src)
                sp.d("gpsimd", store_kt, 1)
                for nh in range(2):
                    def pe(nh=nh):
                        return [nc.tensor.matmul((ps1 if nh == 0 else ps2)[:, :],
                                                 xnT[:, c * 128:(c + 1) * 128],
                                                 M_b[:, c * 1024 + nh * 512: c * 1024 + nh * 512 + 512],
                                                 start=(c == 0), stop=(c == 7)) for c in range(8)]
                    sp.c("tensor", pe)
                    sp.c("scalar", lambda nh=nh: nc.scalar.copy(stage[:, 256 + nh * 512: 256 + nh * 512 + 512], (ps1 if nh == 0 else ps2)[:, :]))
                sp.d("gpsimd", lambda tile=tile: nc.gpsimd.dma_start(out=V_d[tile * 128:tile * 128 + 128, :], in_=stage[:, 256:256 + 1024]), 1)
                for ck in range(2):
                    def pe(ck=ck):
                        return [nc.tensor.matmul(ps0[:, 0:128],
                                                 wq_b[:, c * 256 + ck * 128: c * 256 + ck * 128 + 128],
                                                 xnT[:, c * 128:(c + 1) * 128],
                                                 start=(c == 0), stop=(c == 7)) for c in range(8)]
                    sp.c("tensor", pe)
                    sp.c("scalar", lambda ck=ck: nc.scalar.mul(stage[:, ck * 128:(ck + 1) * 128], ps0[:, 0:128], 1.0 / 16.0))

                def store_qt(tile=tile):
                    src = stage[:, 0:256].rearrange("p (r c) -> p r c", c=128)
                    dst = QT2_d[:, :, tile * 128:tile * 128 + 128].rearrange("r p c -> p r c")
                    return nc.gpsimd.dma_start(out=dst, in_=src)
                sp.d("gpsimd", store_qt, 1)

            def load_ktsb():
                src = KT2_d[:, :, :].rearrange("r p c -> p r c")
                dst = KT_sb[:, :].rearrange("p (r c) -> p r c", c=T)
                return nc.sync.dma_start(out=dst, in_=src)
            sp.d("sync", load_ktsb, 1)

            def load_vsb():
                src = V_d[:, :].rearrange("(tt q) c -> q tt c", q=128)
                dst = V_sb[:, :].rearrange("q (tt c) -> q tt c", c=1024)
                return nc.sync.dma_start(out=dst, in_=src)
            sp.d("sync", load_vsb, 1)

            for qt in range(32):
                def load_qtT(qt=qt):
                    src = QT2_d[:, :, qt * 128:qt * 128 + 128].rearrange("r p c -> p r c")
                    dst = qtT[:, :].rearrange("p (r c) -> p r c", c=128)
                    return nc.sync.dma_start(out=dst, in_=src)
                sp.d("sync", load_qtT, 1)
                for kh in range(2):
                    def pe(kh=kh):
                        out = []
                        for j in range(4):
                            kg = kh * 4 + j
                            for c in range(2):
                                out.append(nc.tensor.matmul(pslist[j][:, :],
                                                            qtT[:, c * 128:(c + 1) * 128],
                                                            KT_sb[:, c * T + kg * 512: c * T + kg * 512 + 512],
                                                            start=(c == 0), stop=(c == 1)))
                        return out
                    sp.c("tensor", pe)

                    def ev(kh=kh):
                        return [nc.scalar.copy(S_f[:, (kh * 4 + j) * 512:(kh * 4 + j) * 512 + 512], pslist[j][:, :]) for j in range(4)]
                    sp.c("scalar", ev)
                for kg in range(8):
                    ks = kg * 512
                    sp.c("vector", lambda qt=qt, ks=ks: nc.vector.tensor_scalar(msk5[:, :], iob[:, ks:ks + 512], qix[:, qt:qt + 1], None, ALU.is_le))
                    sp.c("vector", lambda ks=ks: nc.vector.tensor_mul(S_f[:, ks:ks + 512], S_f[:, ks:ks + 512], msk5[:, :]))
                    sp.c("vector", lambda: nc.vector.tensor_scalar(tmp5[:, :], msk5[:, :], 1e38, -1e38, ALU.mult, ALU.add))
                    sp.c("vector", lambda ks=ks: nc.vector.tensor_add(S_f[:, ks:ks + 512], S_f[:, ks:ks + 512], tmp5[:, :]))
                softmax_rows(sp, nc, A_bf[:, :], S_f[:, :], t)
                transpose_to(sp, nc, lambda j: ATb[:, j * 128:(j + 1) * 128],
                             lambda j: A_bf[:, j * 128:(j + 1) * 128], 32, tbk)
                for nh in range(2):
                    def pe(nh=nh):
                        return [nc.tensor.matmul((ps0 if nh == 0 else ps1)[:, :],
                                                 ATb[:, kt * 128:(kt + 1) * 128],
                                                 V_sb[:, kt * 1024 + nh * 512: kt * 1024 + nh * 512 + 512],
                                                 start=(kt == 0), stop=(kt == 31)) for kt in range(32)]
                    sp.c("tensor", pe)
                    sp.c("scalar", lambda nh=nh: nc.scalar.copy(yo[:, nh * 512:nh * 512 + 512], (ps0 if nh == 0 else ps1)[:, :]))
                ln_plain(sp, nc, yo[:, :], yo[:, :], D, t)
                sp.d("sync", lambda qt=qt: nc.sync.dma_start(out=xt[:, :], in_=x2_d[qt * 128:qt * 128 + 128, :]), 1)
                sp.c("vector", lambda: nc.vector.tensor_add(yo[:, :], yo[:, :], xt[:, :]))
                sp.d("gpsimd", lambda qt=qt: nc.gpsimd.dma_start(out=x3_d[qt * 128:qt * 128 + 128, :], in_=yo[:, :]), 1)

    _stage_c()

    for tile in range(32):
        r0 = tile * 128
        sp.d("sync", lambda r0=r0: nc.sync.dma_start(out=xt[:, :], in_=x3_d[r0:r0 + 128, :]), 1)
        ln_plain(sp, nc, xn[:, :], xt[:, :], D, t)
        sp.d("gpsimd", lambda r0=r0: nc.gpsimd.dma_start(out=h0_d[r0:r0 + 128, :], in_=xn[:, :]), 1)

    def _stage_e():
        with contextlib.ExitStack() as sl:
            sb = lambda n, s, dt=F32: sl.enter_context(nc.sbuf_tensor(n, s, dt))
            arena = sb("E_arena", [128, 43680], BF16)
            WO_OFF = 24576
            w2s = sb("E_w2s", [128, 1024], BF16)
            b1row = sb("E_b1row", [1, 2 * DI_], BF16)
            b2row = sb("E_b2row", [1, D], BF16)
            lngB = sb("E_lngB", [128, D])
            lnbB = sb("E_lnbB", [128, D])
            qsb = sb("E_qsb", [128, 64])
            ksb = sb("E_ksb", [128, 64])
            io5 = sb("E_io5", [128, 512])
            colA = sb("E_colA", [128, 32])
            colB = sb("E_colB", [128, 32])
            tA = sb("E_tA", [128, 64])
            tB = sb("E_tB", [128, 64])
            xsb = sb("E_xsb", [128, D], BF16)
            qf = sb("E_qf", [128, D])
            kf = sb("E_kf", [128, D])
            rot = sb("E_rot", [128, D])
            rtmp = sb("E_rtmp", [128, D])
            red = sb("E_red", [128, 16])
            stage = sb("E_stage", [128, 1024], BF16)
            qtT8 = sb("E_qtT8", [128, 8 * 128], BF16)
            kt512 = sb("E_kt512", [128, 8 * 512], BF16)
            V4 = sb("E_V4", [128, 4 * 1024], BF16)
            msk5 = sb("E_msk5", [128, 512])
            tmp5 = sb("E_tmp5", [128, 512])
            addm5 = sb("E_addm5", [128, 512])
            S5 = sb("E_S5", [128, 512])
            A5 = sb("E_A5", [128, 512], BF16)
            AT5 = sb("E_AT5", [128, 512], BF16)
            YT8 = sb("E_YT8", [128, 8 * 128], BF16)
            sb_g = sb("E_sb_g", [128, 512])
            mid_bf = sb("E_mid_bf", [128, DI_], BF16)
            midT = sb("E_midT", [128, 22 * 128], BF16)

            sp.d("sync", lambda: [nc.sync.dma_start(out=colA[:, :], in_=colAd[:, :]),
                                  nc.sync.dma_start(out=colB[:, :], in_=colBd[:, :])], 2)
            sp.d("sync", lambda: nc.sync.dma_start(out=wtmp[0:1, 0:512], in_=iota5d[:, :]), 1)
            bcast_row(sp, nc, io5[:, :], wtmp, 512, ones_f, ps_m)

            for layer in range(6):
                h_in = h0_d if layer % 2 == 0 else h1_d
                h_out = h1_d if layer % 2 == 0 else h0_d
                sp.d("sync", lambda layer=layer: nc.sync.dma_start(out=wtmp[0:1, 0:64], in_=qscA[layer:layer + 1, :]), 1)
                bcast_row(sp, nc, qsb[:, :], wtmp, 64, ones_f, ps_m)
                sp.d("sync", lambda layer=layer: nc.sync.dma_start(out=wtmp[0:1, 0:64], in_=kscA[layer:layer + 1, :]), 1)
                bcast_row(sp, nc, ksb[:, :], wtmp, 64, ones_f, ps_m)
                sp.d("sync", lambda layer=layer: nc.sync.dma_start(out=wtmp[0:1, 0:D], in_=lngA[layer:layer + 1, :]), 1)
                bcast_row(sp, nc, lngB[:, :], wtmp, D, ones_f, ps_m)
                sp.d("sync", lambda layer=layer: nc.sync.dma_start(out=wtmp[0:1, 0:D], in_=lnbA[layer:layer + 1, :]), 1)
                bcast_row(sp, nc, lnbB[:, :], wtmp, D, ones_f, ps_m)
                for hf in range(3):
                    n0, n1 = hf * 1820, min((hf + 1) * 1820, 2 * DI_)
                    sp.d("sync", lambda layer=layer, n0=n0, n1=n1: nc.sync.dma_start(out=wtmp[0:1, 0:n1 - n0], in_=b1A[layer:layer + 1, n0:n1]), 1)
                    sp.c("vector", lambda n0=n0, n1=n1: nc.vector.tensor_copy(b1row[0:1, n0:n1], wtmp[0:1, 0:n1 - n0]))
                sp.d("sync", lambda layer=layer: nc.sync.dma_start(out=wtmp[0:1, 0:D], in_=b2A[layer:layer + 1, :]), 1)
                sp.c("vector", lambda: nc.vector.tensor_copy(b2row[0:1, :], wtmp[0:1, 0:D]))
                load_w_bf16(sp, nc, arena, WqkvA[layer, :, :], D, 3072, wtmp, 0, nsplit=2)
                load_w_bf16(sp, nc, arena, WoA[layer, :, :], D, D, wtmp, WO_OFF)

                for tile in range(32):
                    r0 = tile * 128
                    rm = max(r0 - 1, 0)

                    def load_shift(r0=r0, rm=rm, h_in=h_in):
                        return [nc.sync.dma_start(out=xt[:, 0:512], in_=h_in[r0:r0 + 128, 0:512]),
                                nc.sync.dma_start(out=xt[:, 512:1024], in_=h_in[rm:rm + 128, 512:1024])]
                    sp.d("sync", load_shift, 2)
                    if tile == 0:
                        sp.c("vector", lambda: nc.vector.memset(xt[0:1, 512:1024], 0.0))
                    sp.c("vector", lambda: nc.vector.tensor_copy(xsb[:, :], xt[:, :]))
                    transpose_to(sp, nc, lambda j: xnT[:, j * 128:(j + 1) * 128],
                                 lambda j: xsb[:, j * 128:(j + 1) * 128], 8, tbk)
                    for grp in range(6):
                        def pe(grp=grp):
                            return [nc.tensor.matmul((ps0 if grp % 2 == 0 else ps1)[:, :],
                                                     xnT[:, c * 128:(c + 1) * 128],
                                                     arena[:, c * 3072 + grp * 512: c * 3072 + grp * 512 + 512],
                                                     start=(c == 0), stop=(c == 7)) for c in range(8)]
                        sp.c("tensor", pe)
                        if grp < 2:
                            sp.c("scalar", lambda grp=grp: nc.scalar.copy(qf[:, grp * 512:grp * 512 + 512], (ps0 if grp % 2 == 0 else ps1)[:, :]))
                        elif grp < 4:
                            sp.c("scalar", lambda grp=grp: nc.scalar.copy(kf[:, (grp - 2) * 512:(grp - 2) * 512 + 512], (ps0 if grp % 2 == 0 else ps1)[:, :]))
                        else:
                            sp.c("scalar", lambda grp=grp: nc.scalar.copy(stage[:, (grp - 4) * 512:(grp - 4) * 512 + 512], (ps0 if grp % 2 == 0 else ps1)[:, :]))
                    sp.d("gpsimd", lambda r0=r0: nc.gpsimd.dma_start(out=V_d[r0:r0 + 128, :], in_=stage[:, :]), 1)

                    def qk_process(qk, scb, tA_d, tB_d, out_d, r0=r0):
                        sp.c("vector", lambda qk=qk: nc.vector.tensor_mul(t['xc'][:, :], qk[:, :], qk[:, :]))

                        def red_f():
                            xv = t['xc'][:, :].rearrange("p (h j) -> p h j", j=64)
                            return nc.vector.reduce_sum(red[:, :].rearrange("p (h j) -> p h j", j=1), xv, axis=AX.X)
                        sp.c("vector", red_f)
                        sp.c("scalar", lambda: nc.scalar.activation(red[:, :], red[:, :], AF.Sqrt, scale=1.0))
                        sp.c("vector", lambda: nc.vector.tensor_single_scalar(red[:, :], red[:, :], 1e-12, ALU.max))
                        sp.c("vector", lambda: nc.vector.reciprocal(red[:, :], red[:, :]))

                        def nrm(qk=qk):
                            qv = qk[:, :].rearrange("p (h j) -> p h j", j=64)
                            a, b = broadcast_tensor_aps(qv, red[:, :].rearrange("p (h j) -> p h j", j=1))
                            return nc.vector.tensor_tensor(qv, a, b, ALU.mult)
                        sp.c("vector", nrm)
                        head_bcast_mul(sp, nc, qk[:, :], scb, 16, 64)
                        sp.d("sync", lambda tA_d=tA_d, tB_d=tB_d, r0=r0: [nc.sync.dma_start(out=tA[:, :], in_=tA_d[r0:r0 + 128, :]),
                                                                         nc.sync.dma_start(out=tB[:, :], in_=tB_d[r0:r0 + 128, :])], 2)
                        rope_apply(sp, nc, qk[:, :], rot, tA, tB, 16, 64, rtmp)
                        sp.c("vector", lambda qk=qk: nc.vector.tensor_copy(xsb[:, :], qk[:, :]))
                        transpose_to(sp, nc, lambda j: stage[:, j * 128:(j + 1) * 128],
                                     lambda j: xsb[:, j * 128:(j + 1) * 128], 8, tbk)

                        def store(r0=r0, out_d=out_d):
                            src = stage[:, 0:1024].rearrange("p (r c) -> p r c", c=128)
                            dst = out_d[:, :, r0:r0 + 128].rearrange("r p c -> p r c")
                            return nc.gpsimd.dma_start(out=dst, in_=src)
                        sp.d("gpsimd", store, 1)

                    qk_process(qf, qsb, tabA_d, tabB_d, QT_d)
                    qk_process(kf, ksb, tabAi_d, tabBi_d, KT_d)

                for qt in range(32):
                    w = qt // 2
                    k0 = max((w - 1) * 256, 0)

                    def ld_q(qt=qt):
                        src = QT_d[:, :, qt * 128:qt * 128 + 128].rearrange("r p c -> p r c")
                        dst = qtT8[:, :].rearrange("p (r c) -> p r c", c=128)
                        return nc.sync.dma_start(out=dst, in_=src)
                    sp.d("sync", ld_q, 1)

                    def ld_k(k0=k0):
                        src = KT_d[:, :, k0:k0 + 512].rearrange("r p c -> p r c")
                        dst = kt512[:, :].rearrange("p (r c) -> p r c", c=512)
                        return nc.sync.dma_start(out=dst, in_=src)
                    sp.d("sync", ld_k, 1)

                    def ld_v(k0=k0):
                        src = V_d[k0:k0 + 512, :].rearrange("(kt p) c -> p kt c", p=128)
                        dst = V4[:, :].rearrange("p (kt c) -> p kt c", c=1024)
                        return nc.sync.dma_start(out=dst, in_=src)
                    sp.d("sync", ld_v, 1)
                    sp.c("vector", lambda qt=qt: nc.vector.tensor_scalar(msk5[:, :], io5[:, :], colA[:, qt:qt + 1], None, ALU.is_ge))
                    sp.c("vector", lambda qt=qt: nc.vector.tensor_scalar(tmp5[:, :], io5[:, :], colB[:, qt:qt + 1], None, ALU.is_le))
                    sp.c("vector", lambda: nc.vector.tensor_mul(msk5[:, :], msk5[:, :], tmp5[:, :]))
                    sp.c("vector", lambda: nc.vector.tensor_scalar(addm5[:, :], msk5[:, :], 1e38, -1e38, ALU.mult, ALU.add))
                    for h in range(16):
                        hp, hh = h // 2, h % 2
                        hb = hh * 64

                        def pe_s(hp=hp, hb=hb):
                            return nc.tensor.matmul(ps0[:, :], qtT8[hb:hb + 64, hp * 128:hp * 128 + 128],
                                                    kt512[hb:hb + 64, hp * 512:hp * 512 + 512], start=True, stop=True)
                        sp.c("tensor", pe_s)
                        sp.c("vector", lambda: nc.vector.tensor_mul(S5[:, :], ps0[:, :], msk5[:, :]))
                        sp.c("vector", lambda: nc.vector.tensor_add(S5[:, :], S5[:, :], addm5[:, :]))
                        softmax_rows(sp, nc, A5[:, :], S5[:, :], t)

                        def pe_t():
                            return [nc.tensor.transpose(ps_tb[:, i * 128:(i + 1) * 128], A5[:, i * 128:(i + 1) * 128], identb[:, :])
                                    for i in range(4)]
                        sp.c("tensor", pe_t)
                        sp.c("vector", lambda: nc.vector.tensor_copy(AT5[:, :], ps_tb[:, :]))

                        def pe_av(h=h):
                            return [nc.tensor.matmul(ps_y[0:64, 0:128], V4[:, kt * 1024 + h * 64: kt * 1024 + h * 64 + 64],
                                                     AT5[:, kt * 128:(kt + 1) * 128], start=(kt == 0), stop=(kt == 3)) for kt in range(4)]
                        sp.c("tensor", pe_av)
                        sp.c("scalar", lambda hb=hb, hp=hp: nc.scalar.copy(YT8[hb:hb + 64, hp * 128:hp * 128 + 128], ps_y[0:64, 0:128]))
                    for nh in range(2):
                        def pe(nh=nh):
                            return [nc.tensor.matmul((ps0 if nh == 0 else ps1)[:, :],
                                                     YT8[:, c * 128:(c + 1) * 128],
                                                     arena[:, WO_OFF + c * 1024 + nh * 512: WO_OFF + c * 1024 + nh * 512 + 512],
                                                     start=(c == 0), stop=(c == 7)) for c in range(8)]
                        sp.c("tensor", pe)
                        sp.c("scalar", lambda nh=nh: nc.scalar.copy(yo[:, nh * 512:nh * 512 + 512], (ps0 if nh == 0 else ps1)[:, :]))
                    sp.d("sync", lambda qt=qt, h_in=h_in: nc.sync.dma_start(out=xt[:, :], in_=h_in[qt * 128:qt * 128 + 128, :]), 1)
                    sp.c("vector", lambda: nc.vector.tensor_add(yo[:, :], yo[:, :], xt[:, :]))
                    sp.d("gpsimd", lambda qt=qt: nc.gpsimd.dma_start(out=y_d[qt * 128:qt * 128 + 128, :], in_=yo[:, :]), 1)

                load_w_bf16(sp, nc, arena, W1A[layer, :, :], D, 2 * DI_, wtmp, 0, nsplit=3)
                NG = [(j * 512, min((j + 1) * 512, DI_)) for j in range((DI_ + 511) // 512)]
                for tile in range(32):
                    r0 = tile * 128
                    rm = max(r0 - 1, 0)

                    def load_shift(r0=r0, rm=rm):
                        return [nc.sync.dma_start(out=xt[:, 0:512], in_=y_d[r0:r0 + 128, 0:512]),
                                nc.sync.dma_start(out=xt[:, 512:1024], in_=y_d[rm:rm + 128, 512:1024])]
                    sp.d("sync", load_shift, 2)
                    if tile == 0:
                        sp.c("vector", lambda: nc.vector.memset(xt[0:1, 512:1024], 0.0))
                    ln_plain(sp, nc, t['xc'][:, :], xt[:, :], D, t)
                    sp.c("vector", lambda: nc.vector.tensor_mul(t['xc'][:, :], t['xc'][:, :], lngB[:, :]))
                    sp.c("vector", lambda: nc.vector.tensor_add(t['xc'][:, :], t['xc'][:, :], lnbB[:, :]))
                    sp.c("vector", lambda: nc.vector.tensor_copy(xsb[:, :], t['xc'][:, :]))
                    transpose_to(sp, nc, lambda j: xnT[:, j * 128:(j + 1) * 128],
                                 lambda j: xsb[:, j * 128:(j + 1) * 128], 8, tbk)
                    for (n0, n1) in NG:
                        wdt = n1 - n0

                        def pe_a(n0=n0, n1=n1, wdt=wdt):
                            out = [nc.tensor.matmul(ps0[:, 0:wdt], ones_b[0:1, :], b1row[0:1, n0:n1], start=True, stop=False)]
                            out += [nc.tensor.matmul(ps0[:, 0:wdt], xnT[:, c * 128:(c + 1) * 128],
                                                     arena[:, c * 2 * DI_ + n0: c * 2 * DI_ + n1],
                                                     start=False, stop=(c == 7)) for c in range(8)]
                            return out
                        sp.c("tensor", pe_a)

                        def pe_g(n0=n0, n1=n1, wdt=wdt):
                            out = [nc.tensor.matmul(ps1[:, 0:wdt], ones_b[0:1, :], b1row[0:1, DI_ + n0:DI_ + n1], start=True, stop=False)]
                            out += [nc.tensor.matmul(ps1[:, 0:wdt], xnT[:, c * 128:(c + 1) * 128],
                                                     arena[:, c * 2 * DI_ + DI_ + n0: c * 2 * DI_ + DI_ + n1],
                                                     start=False, stop=(c == 7)) for c in range(8)]
                            return out
                        sp.c("tensor", pe_g)
                        sp.c("scalar", lambda wdt=wdt: nc.scalar.activation(sb_g[:, 0:wdt], ps1[:, 0:wdt], AF.Gelu, scale=1.0))
                        sp.c("vector", lambda n0=n0, n1=n1, wdt=wdt: nc.vector.tensor_mul(mid_bf[:, n0:n1], ps0[:, 0:wdt], sb_g[:, 0:wdt]))
                    nfull = DI_ // 128
                    transpose_to(sp, nc, lambda j: midT[:, j * 128:(j + 1) * 128],
                                 lambda j: mid_bf[:, j * 128:(j + 1) * 128], nfull, tbk)
                    sp.c("tensor", lambda nfull=nfull: nc.tensor.transpose(ps_tb[0:DI_ - nfull * 128, 0:128], mid_bf[:, nfull * 128:DI_], identb[:, :]))
                    sp.c("vector", lambda nfull=nfull: nc.vector.tensor_copy(midT[0:DI_ - nfull * 128, nfull * 128:nfull * 128 + 128], ps_tb[0:DI_ - nfull * 128, 0:128]))
                    for c in range(22):
                        kr = 128 if c < 21 else DI_ - 21 * 128
                        sp.d("sync", lambda layer=layer, c=c, kr=kr: nc.sync.dma_start(out=wtmp[0:kr, 0:1024], in_=W2A[layer, c * 128:c * 128 + kr, :]), 1)
                        sp.c("vector", lambda kr=kr: nc.vector.tensor_copy(w2s[0:kr, :], wtmp[0:kr, 0:1024]))

                        def pe(c=c, kr=kr):
                            out = []
                            for nh in range(2):
                                pp = ps0 if nh == 0 else ps1
                                if c == 0:
                                    out.append(nc.tensor.matmul(pp[:, :], ones_b[0:1, :], b2row[0:1, nh * 512:nh * 512 + 512], start=True, stop=False))
                                out.append(nc.tensor.matmul(pp[:, :], midT[0:kr, c * 128:c * 128 + 128],
                                                            w2s[0:kr, nh * 512:nh * 512 + 512],
                                                            start=False, stop=(c == 21)))
                            return out
                        sp.c("tensor", pe)
                    for nh in range(2):
                        sp.c("scalar", lambda nh=nh: nc.scalar.copy(yo[:, nh * 512:nh * 512 + 512], (ps0 if nh == 0 else ps1)[:, :]))
                    sp.d("sync", lambda r0=r0: nc.sync.dma_start(out=xt[:, :], in_=y_d[r0:r0 + 128, :]), 1)
                    sp.c("vector", lambda: nc.vector.tensor_add(yo[:, :], yo[:, :], xt[:, :]))
                    sp.d("gpsimd", lambda tile=tile, h_out=h_out: nc.gpsimd.dma_start(out=h_out[tile * 128:tile * 128 + 128, :], in_=yo[:, :]), 1)

    _stage_e()

    hfin = h0_d
    for tile in range(32):
        r0 = tile * 128
        sp.d("sync", lambda r0=r0: nc.sync.dma_start(out=xt[:, :], in_=hfin[r0:r0 + 128, :]), 1)
        ln_plain(sp, nc, xn[:, :], xt[:, :], D, t)
        sp.d("sync", lambda r0=r0: nc.sync.dma_start(out=xt[:, :], in_=x3_d[r0:r0 + 128, :]), 1)
        sp.c("vector", lambda: nc.vector.tensor_add(xn[:, :], xn[:, :], xt[:, :]))
        sp.d("gpsimd", lambda r0=r0: nc.gpsimd.dma_start(out=xod[r0:r0 + 128, :], in_=xn[:, :]), 1)

    sp.build()
    st_glob.close()
    return nc


def kernel(**inp):
    f32 = np.float32
    cc = np.ascontiguousarray
    ones_r = np.ones((1, 128), f32)
    ones_rb = np.ones((1, 128), ml_dtypes.bfloat16)
    identb = np.eye(128, dtype=ml_dtypes.bfloat16)
    identf = np.eye(128, dtype=f32)
    x = np.asarray(inp["L0_attn_x"], f32)
    ncF = _get_prog("F", build_fused)
    ropeC, ropeS = _rope_l0()
    A, Bt, Ai, Bi = _xpos_l1()
    iota = np.arange(T, dtype=f32).reshape(1, T)
    iota5 = np.arange(512, dtype=f32).reshape(1, 512)
    qidx = cc(np.arange(T, dtype=f32).reshape(32, 128).T)
    colA = np.zeros((128, 32), f32)
    colB = np.zeros((128, 32), f32)
    p = np.arange(128, dtype=f32)
    for qt in range(32):
        w = qt // 2
        k0 = max((w - 1) * 256, 0)
        qg = qt * 128 + p
        colA[:, qt] = qg - 256 - k0
        colB[:, qt] = qg - k0
    maps = []
    for c in range(NCORE):
        b = c // 2
        maps.append(dict(
            xb=cc(x[b]),
            wq=inp["L0_attn_wq_w"], wk=inp["L0_attn_wk_w"], wv=inp["L0_attn_wv_w"], wo=inp["L0_attn_wo_w"],
            bqr=inp["L0_attn_wq_b"].reshape(1, D), bkr=inp["L0_attn_wk_b"].reshape(1, D),
            bvr=inp["L0_attn_wv_b"].reshape(1, D), bor=inp["L0_attn_wo_b"].reshape(1, D),
            ropeCq=cc(ropeC * 0.125), ropeSq=cc(ropeS * 0.125), ropeCk=ropeC, ropeSk=ropeS,
            ln_wd=inp["L0_ffn_ln_w"].reshape(1, D), ln_bd=inp["L0_ffn_ln_b"].reshape(1, D),
            router_w=inp["L0_ffn_router_w"], gate_w=inp["L0_ffn_gate_w"], up_w=inp["L0_ffn_up_w"],
            down_w=inp["L0_ffn_down_w"], sh_gate=inp["L0_ffn_sh_gate"], sh_up=inp["L0_ffn_sh_up"],
            sh_down=inp["L0_ffn_sh_down"],
            Wqd=inp["L1_attn_Wq"], Wkd=inp["L1_attn_Wk"],
            WvdT=cc(np.asarray(inp["L1_attn_Wvdown"]).T), Wvu=inp["L1_attn_Wvup"],
            iotad=iota, qidxd=qidx,
            WqkvA=inp["L1_ffn_Wqkv"], WoA=inp["L1_ffn_Wo"], W1A=inp["L1_ffn_W1"], b1A=inp["L1_ffn_b1"],
            W2A=inp["L1_ffn_W2"], b2A=inp["L1_ffn_b2"], lngA=inp["L1_ffn_ln_g"], lnbA=inp["L1_ffn_ln_b"],
            qscA=inp["L1_ffn_q_scale"], kscA=inp["L1_ffn_k_scale"],
            tabA_d=A, tabB_d=Bt, tabAi_d=Ai, tabBi_d=Bi,
            iota5d=iota5, colAd=colA, colBd=colB,
            onesd=ones_r, onesbd=ones_rb, identd=identb, identfd=identf))
    res = _run(ncF, maps)
    x4 = np.stack([res[2 * b]["xod"] for b in range(B)], 0)
    return x4
